# version 27
# baseline (speedup 1.0000x reference)
import numpy as np
import ml_dtypes
from contextlib import ExitStack
import concourse.bass as bass
import concourse.mybir as mybir
from concourse.bass import IndirectOffsetOnAxis
from concourse.bass_utils import run_bass_kernel_spmd

F32 = mybir.dt.float32
BF16 = mybir.dt.bfloat16
I32 = mybir.dt.int32
U32 = mybir.dt.uint32
AF = mybir.ActivationFunctionType
OP = mybir.AluOpType
AX = mybir.AxisListType.X

FULL = dict(D=2048, NN=2048, TCORE=1024, I_EXP=1408, IS=2816, CAP=1024, E=16, NCORE=8)
TINY = dict(D=512, NN=512, TCORE=256, I_EXP=256, IS=512, CAP=256, E=16, NCORE=8)
EPS = 1e-5


def build(cfg):
    D = cfg["D"]; NN = cfg["NN"]; TCORE = cfg["TCORE"]; I_EXP = cfg["I_EXP"]
    IS = cfg["IS"]; CAP = cfg["CAP"]; E = cfg["E"]; NCORE = cfg["NCORE"]
    NTC = TCORE // 128
    NKD = D // 128
    NNC = NN // 128
    NKI = I_EXP // 128
    NKIS = IS // 128
    NTB = max(1, TCORE // 512)
    TBS = TCORE // NTB
    NSB = max(1, CAP // 512)
    SBS = CAP // NSB
    ND4 = D // 512
    NH = 2 if (ND4 % 2 == 0 and ND4 >= 2) else 1
    DH = D // NH
    EPC = E // NCORE
    NSC = CAP // 128
    TTOK = NCORE * TCORE
    MROW = E * CAP // 128
    WRN = 5

    nc = bass.Bass("TRN2", target_bir_lowering=False)

    def dram_in(name, shape, dt=F32):
        return nc.dram_tensor(name, shape, dt, kind="ExternalInput").ap()

    x_in = dram_in("x_in", [TCORE, D])
    wmix = {m: dram_in(f"w{m}", [NNC, 128, NKD, 128]) for m in ("A", "B", "D", "Ca", "Cb")}
    wo_in = dram_in("wo", [NNC, 128, D])
    Ab_in = dram_in("Ab", [128, NNC])
    rms_in = dram_in("rmsr", [128, D])
    outb_in = dram_in("outbr", [128, D])
    gate_in = dram_in("gatet", [128, NKD, E])
    wsg_in = dram_in("wsg", [NKIS, 128, NKD, 128])
    wsu_in = dram_in("wsu", [NKIS, 128, NKD, 128])
    wsd_in = dram_in("wsd", [NKIS, 128, D])
    wg_in = [dram_in(f"wg{j}", [NKI, 128, NKD, 128]) for j in range(EPC)]
    wu_in = [dram_in(f"wu{j}", [NKI, 128, NKD, 128]) for j in range(EPC)]
    wd_in = [dram_in(f"wd{j}", [NKI, 128, D]) for j in range(EPC)]
    lt_in = dram_in("lt", [128, 128], BF16)
    idn_in = dram_in("idn", [128, 128], BF16)
    idnf_in = dram_in("idnf", [128, 128], F32)
    tokb_in = dram_in("tokb", [128, 1])
    mapb_in = dram_in("mapb", [128, 1])
    pm_in = dram_in("pmr", [128, NCORE])
    apm_in = dram_in("apmr", [128, NCORE])

    y_out = nc.dram_tensor("y_out", [TCORE, D], F32, kind="ExternalOutput").ap()

    ex_in_a = nc.dram_tensor("ex_in_a", [128, NNC], F32).ap()
    ex_in_b = nc.dram_tensor("ex_in_b", [128, NNC], F32).ap()
    ex_out_a = nc.dram_tensor("ex_out_a", [NCORE * 128, NNC], F32, addr_space="Shared").ap()
    ex_out_b = nc.dram_tensor("ex_out_b", [NCORE * 128, NNC], F32, addr_space="Shared").ap()
    cnt_in = nc.dram_tensor("cnt_in", [1, E], F32).ap()
    cnt_out = nc.dram_tensor("cnt_out", [NCORE, E], F32, addr_space="Shared").ap()
    map_in_t = nc.dram_tensor("map_in", [4 * MROW, 128], F32).ap()
    map_out_t = nc.dram_tensor("map_out", [MROW, 128], F32).ap()
    ag1_in = nc.dram_tensor("ag1_in", [TCORE, D], BF16).ap()
    ag1_out = nc.dram_tensor("ag1_out", [TTOK, D], BF16, addr_space="Shared").ap()
    ag2_in = nc.dram_tensor("ag2_in", [NH * EPC * CAP, DH], BF16).ap()
    ag2_out = nc.dram_tensor("ag2_out", [NH * E * CAP, DH], BF16, addr_space="Shared").ap()
    xr1_sp = nc.dram_tensor("xr1_sp", [TCORE, D], F32).ap()

    with ExitStack() as ctx:
        def sbt(name, shape, dt):
            return ctx.enter_context(nc.sbuf_tensor(name, shape, dt))

        bigA = sbt("bigA", [128, max(NKD * TCORE, NKI * CAP)], BF16)
        bigB = sbt("bigB", [128, max(NNC, NKIS) * TCORE], BF16)
        bigC = sbt("bigC", [128, max(NNC * TCORE, NKD * CAP)], BF16)
        sx = sbt("sx", [128, 2 * D], F32)
        un32 = sbt("un32", [128, max(5 * TCORE, 2 * D)], F32)
        zeros = sbt("zeros", [128, max(TCORE, 128)], F32)
        WSL = max(NKD, NKIS, NKI) * 128
        wring = sbt("wring", [128, WRN * WSL], BF16)
        rmsr = sbt("rmsr_s", [128, D], BF16)
        outbr = sbt("outbr_s", [128, D], BF16)
        xb16 = sbt("xb16", [128, 2 * D], BF16)
        slt = sbt("slt", [128, 128], BF16)
        sidn = sbt("sidn", [128, 128], BF16)
        sidnf = sbt("sidnf", [128, 128], F32)
        sones = sbt("sones", [128, 128], BF16)
        sgateb = sbt("sgateb", [128, NKD * E], BF16)
        sAb = sbt("sAb", [128, NNC], F32)
        sm32 = sbt("sm32", [128, 768], F32)
        s_rout = sbt("s_rout", [128, 8 * NTC], F32)
        s_scat = sbt("s_scat", [128, 2 * NTC], I32)
        s_gidx = sbt("s_gidx", [128, 2 * NTC], I32)
        s_gidx2 = sbt("s_gidx2", [128, 2 * NTC], I32)
        s_sval = sbt("s_sval", [128, NTC], F32)
        s_ex = sbt("s_ex", [128, NCORE * 2 * NNC], F32)
        s_map16 = sbt("s_map16", [EPC * NSC, 128], F32)
        s_mapT = sbt("s_mapT", [128, EPC * NSC], F32)
        s_migx = sbt("s_migx", [128, EPC * NSC], I32)
        s_m16 = sbt("s_m16", [EPC * NSC, 1], I32)
        s_m16b = sbt("s_m16b", [EPC * NSC, 1], I32)
        smagic = sbt("smagic", [128, 1], I32)
        siota = sbt("siota", [128, E], F32)
        spid = sbt("spid", [128, 1], F32)
        scnt = sbt("scnt", [1, NCORE * E], F32)
        g_bf = sbt("g_bf", [128, 2 * D], BF16)
        gdump = g_bf.ap().bitcast(F32)[:, 0:D]
        pm_s = sbt("pm_s", [128, NCORE], F32)
        apm_s = sbt("apm_s", [128, NCORE], F32)
        tokb_s = sbt("tokb_s", [128, 1], F32)
        mapb_s = sbt("mapb_s", [128, 1], F32)

        _smoff = [0]; _smmap = {}
        def sm(name, w=1):
            if name not in _smmap:
                _smmap[name] = (_smoff[0], w); _smoff[0] += w
                assert _smoff[0] <= 768, "sm32 overflow"
            o, ww = _smmap[name]
            assert ww == w
            return sm32[:, o:o + w]

        U = un32
        def u32(off, w):
            return U[:, off:off + w]

        pf = [ctx.enter_context(nc.psum_tensor(f"pf{i}", [128, 512], F32)) for i in range(8)]
        pbf = [pf[6].ap().bitcast(BF16), pf[7].ap().bitcast(BF16)]

        semn = (["mm", "ve", "act", "pcm", "cc", "setup", "x0", "x1",
                 "spl", "ag1s", "ag2s", "exr", "outp", "mz", "mz2", "setupg"]
                + [f"wt{i}" for i in range(WRN)])
        sem = {n: ctx.enter_context(nc.semaphore("s_" + n)) for n in semn}
        cnt = {n: 0 for n in semn}
        Q = {k: [] for k in ("tensor", "vector", "scalar", "gpsimd", "sync")}

        def wait(eng, s, v):
            if v > 0:
                Q[eng].append(lambda e, s=s, v=v: e.wait_ge(sem[s], v))

        def bump(s, k=1):
            cnt[s] += k
            return cnt[s]

        def drain(eng):
            Q[eng].append(lambda e: e.drain())

        pool_drain = [False]

        def emit(eng, fn, inc=None, inck=1):
            if inc is None:
                Q[eng].append(lambda e, f=fn: f(e))
            else:
                bump(inc, inck)
                Q[eng].append(lambda e, f=fn, s=inc, k=inck: f(e).then_inc(sem[s], k))
            if eng in ("vector", "scalar") or (eng == "gpsimd" and pool_drain[0]):
                Q[eng].append(lambda e: e.drain())

        def sdma(stream, eng, fn):
            wait(eng, stream, cnt[stream])
            emit(eng, fn, stream, 16)
            return (stream, cnt[stream])

        def cc_issue(fn):
            wait("gpsimd", "cc", cnt["cc"])
            emit("gpsimd", fn, "cc", 1)
            return cnt["cc"]

        bank_free = {b: None for b in range(8)}

        def pe_take_bank(b):
            if bank_free[b] is not None:
                s, v = bank_free[b]
                wait("tensor", s, v)
                bank_free[b] = None

        def consumer_frees(b, s, v):
            bank_free[b] = (s, v)

        ring = {"seq": 0, "cons": {}}

        def load_slice(src_ap, width):
            s = ring["seq"]; ring["seq"] += 1
            slot = s % WRN
            if s >= WRN:
                c = ring["cons"].get(s - WRN)
                assert c is not None, f"ring slot {s} reused before consumer set"
                wait("gpsimd", c[0], c[1])
            dst = wring[:, slot * WSL:slot * WSL + width]
            mark = sdma(f"wt{slot}", "gpsimd",
                        lambda e, d=dst, sa=src_ap: e.dma_start(out=d, in_=sa))
            return s, dst, mark

        def slice_consumed(seq, s, v):
            ring["cons"][seq] = (s, v)

        for dst, src in [(slt, lt_in), (sidn, idn_in),
                         (sidnf, idnf_in), (sAb, Ab_in), (tokb_s, tokb_in),
                         (mapb_s, mapb_in), (pm_s, pm_in), (apm_s, apm_in)]:
            emit("sync", lambda e, d=dst, s=src: e.dma_start(out=d[:, :], in_=s),
                 "setup", 16)
        setup_t = cnt["setup"]
        for dst, src in [(rmsr, rms_in), (outbr, outb_in)]:
            emit("gpsimd", lambda e, d=dst, s=src: e.dma_start(out=d[:, :], in_=s),
                 "setupg", 16)
        emit("gpsimd", lambda e: e.dma_start(
            out=sgateb[:, :], in_=gate_in.rearrange("p k e -> p (k e)")), "setupg", 16)
        setup_g = cnt["setupg"]

        wait("vector", "setup", setup_t)
        wait("vector", "setupg", setup_g)
        emit("vector", lambda e: e.memset(zeros[:, :], 0.0))
        emit("vector", lambda e: e.memset(smagic[:, :], 0x5F3759DF))
        emit("vector", lambda e: e.memset(sones[:, :], 1.0), "ve")
        ve_setup = cnt["ve"]
        wait("gpsimd", "ve", ve_setup)
        pool_drain[0] = True
        emit("gpsimd", lambda e: e.iota(out=siota[:, :], pattern=[[1, E]], base=0,
                                        channel_multiplier=0,
                                        allow_small_or_imprecise_dtypes=True))
        emit("gpsimd", lambda e: e.iota(out=spid[:, :], pattern=[[0, 1]], base=0,
                                        channel_multiplier=1,
                                        allow_small_or_imprecise_dtypes=True))
        emit("gpsimd", lambda e: e.iota(out=s_m16[:, :], pattern=[[0, 1]], base=0,
                                        channel_multiplier=1,
                                        allow_small_or_imprecise_dtypes=True))
        emit("gpsimd", lambda e: e.iota(out=s_sval[:, :], pattern=[[128, NTC]], base=0,
                                        channel_multiplier=1,
                                        allow_small_or_imprecise_dtypes=True), "pcm")
        pool_drain[0] = False
        pcm_setup = cnt["pcm"]
        wait("vector", "pcm", pcm_setup)
        emit("vector", lambda e: e.tensor_scalar(out=spid[:, :], in0=spid[:, :],
                                                 scalar1=tokb_s[:, 0:1], scalar2=None,
                                                 op0=OP.add))
        emit("vector", lambda e: e.tensor_scalar(out=s_sval[:, :], in0=s_sval[:, :],
                                                 scalar1=tokb_s[:, 0:1], scalar2=None,
                                                 op0=OP.add))
        emit("vector", lambda e: e.tensor_copy(out=s_m16b[:, :],
                                               in_=mapb_s[0:EPC * NSC, 0:1]))
        drain("vector")
        emit("vector", lambda e: e.tensor_tensor(out=s_m16[:, :], in0=s_m16[:, :],
                                                 in1=s_m16b[:, :], op=OP.add), "ve")
        drain("vector")
        m16_done = cnt["ve"]

        def newton_rsqrt(dst, src, eng="vector", pfx=""):
            t1, ti = sm(pfx + "nt1"), sm(pfx + "nti")
            emit(eng, lambda e: e.tensor_scalar(out=ti.bitcast(I32),
                                                in0=src.bitcast(I32),
                                                scalar1=1, scalar2=None,
                                                op0=OP.logical_shift_right))
            drain(eng)
            emit(eng, lambda e: e.tensor_scalar(out=ti.bitcast(I32),
                                                in0=ti.bitcast(I32),
                                                scalar1=-1, scalar2=None,
                                                op0=OP.bitwise_xor))
            drain(eng)
            emit(eng, lambda e: e.tensor_scalar(out=dst.bitcast(I32),
                                                in0=ti.bitcast(I32),
                                                scalar1=0x5F3759DF + 1, scalar2=None,
                                                op0=OP.add))
            drain(eng)
            for _ in range(2):
                emit(eng, lambda e: e.tensor_tensor(out=t1, in0=dst, in1=dst,
                                                    op=OP.mult))
                drain(eng)
                emit(eng, lambda e: e.tensor_tensor(out=t1, in0=t1, in1=src,
                                                    op=OP.mult))
                drain(eng)
                emit(eng, lambda e: e.tensor_scalar(out=t1, in0=t1, scalar1=-0.5,
                                                    scalar2=1.5, op0=OP.mult,
                                                    op1=OP.add))
                drain(eng)
                emit(eng, lambda e: e.tensor_tensor(out=dst, in0=dst, in1=t1,
                                                    op=OP.mult))
                drain(eng)

        def ln_stats(src_ap, dump_ap, s_dst, with_rms=False):
            wait("scalar", "ve", cnt["ve"])
            emit("scalar", lambda e: e.activation(out=dump_ap, in_=src_ap, func=AF.Copy,
                                                  accum_out=sm("aS")))
            drain("scalar")
            emit("scalar", lambda e: e.activation(out=dump_ap, in_=src_ap, func=AF.Square,
                                                  accum_out=sm("aSS")), "act")
            wait("vector", "act", cnt["act"])
            emit("vector", lambda e: e.tensor_scalar(out=sm("mS"), in0=sm("aS"),
                                                     scalar1=1.0 / D, scalar2=None,
                                                     op0=OP.mult))
            drain("vector")
            emit("vector", lambda e: e.tensor_scalar(out=sm("eS"), in0=sm("aSS"),
                                                     scalar1=1.0 / D, scalar2=None,
                                                     op0=OP.mult))
            drain("vector")
            emit("vector", lambda e: e.tensor_tensor(out=sm("m2S"), in0=sm("mS"),
                                                     in1=sm("mS"), op=OP.mult))
            drain("vector")
            emit("vector", lambda e: e.tensor_tensor(out=sm("vS"), in0=sm("eS"),
                                                     in1=sm("m2S"), op=OP.subtract))
            drain("vector")
            if with_rms:
                emit("vector", lambda e: e.tensor_scalar(out=sm("qS"), in0=sm("vS"),
                                                         scalar1=1.0 + EPS,
                                                         scalar2=EPS * EPS,
                                                         op0=OP.mult, op1=OP.add))
            else:
                emit("vector", lambda e: e.tensor_scalar(out=sm("qS"), in0=sm("vS"),
                                                         scalar1=1.0, scalar2=EPS,
                                                         op0=OP.mult, op1=OP.add))
            drain("vector")
            newton_rsqrt(s_dst, sm("qS"))

        def stats_fin_g(aname, ssname, dst, rms=False, eng="vector", pfx="",
                        mdst=None):
            mS = sm(pfx + "mS") if mdst is None else mdst
            emit(eng, lambda e: e.tensor_scalar(
                out=mS, in0=sm(aname), scalar1=1.0 / D, scalar2=None,
                op0=OP.mult))
            emit(eng, lambda e: e.tensor_scalar(
                out=sm(pfx + "eS"), in0=sm(ssname), scalar1=1.0 / D, scalar2=None,
                op0=OP.mult))
            emit(eng, lambda e: e.tensor_tensor(out=sm(pfx + "m2S"), in0=mS,
                                                in1=mS, op=OP.mult))
            emit(eng, lambda e: e.tensor_tensor(out=sm(pfx + "vS"),
                                                in0=sm(pfx + "eS"),
                                                in1=sm(pfx + "m2S"), op=OP.subtract))
            if rms:
                emit(eng, lambda e: e.tensor_scalar(out=sm(pfx + "qS"),
                                                    in0=sm(pfx + "vS"),
                                                    scalar1=1.0 + EPS,
                                                    scalar2=EPS * EPS,
                                                    op0=OP.mult, op1=OP.add))
            else:
                emit(eng, lambda e: e.tensor_scalar(out=sm(pfx + "qS"),
                                                    in0=sm(pfx + "vS"),
                                                    scalar1=1.0, scalar2=EPS,
                                                    op0=OP.mult, op1=OP.add))
            newton_rsqrt(dst, sm(pfx + "qS"), eng=eng, pfx=pfx)

        tp = {"fill": 0}

        def transpose_tile(src_tile, dst_fn, nblk, pre_wait=None):
            mm_mark = 0
            for half in range((nblk + 7) // 8):
                f = tp["fill"]; tp["fill"] += 1
                pb = f % 2
                pe_take_bank(6 + pb)
                if pre_wait is not None:
                    wait("tensor", *pre_wait)
                    pre_wait = None
                nb = min(8, nblk - half * 8)
                for j in range(nb):
                    dc = half * 8 + j
                    emit("tensor",
                         lambda e, pb=pb, j=j, dc=dc, st=src_tile:
                         e.transpose(out=pbf[pb][:, j * 128:(j + 1) * 128],
                                     in_=st[:, dc * 128:(dc + 1) * 128],
                                     identity=sidn[:, :]),
                         inc="mm" if j == nb - 1 else None)
                mm_mark = cnt["mm"]
                wait("vector", "mm", mm_mark)
                for j in range(nb):
                    dc = half * 8 + j
                    emit("vector",
                         lambda e, pb=pb, j=j, d=dst_fn(dc):
                         e.tensor_copy(out=d, in_=pbf[pb][:, j * 128:(j + 1) * 128]),
                         inc="ve" if j == nb - 1 else None)
                consumer_frees(6 + pb, "ve", cnt["ve"])
            return cnt["ve"], mm_mark

        x_slot_free = {0: ("ve", 0), 1: ("ve", 0)}
        xb_slot_free = {0: 0, 1: 0}
        m0_sa = {}
        m0_xm = {}
        tpve_k = {}

        def m0_x(k):
            par = k % 2
            wait("sync", *x_slot_free[par])
            m0_xm[k] = sdma(f"x{par}", "sync",
                            lambda e, k=k, par=par: e.dma_start(
                                out=sx[:, par * D:(par + 1) * D],
                                in_=x_in[k * 128:(k + 1) * 128, :]))

        def m0_a(k):
            par = k % 2
            xt = sx[:, par * D:(par + 1) * D]
            wait("scalar", *m0_xm[k])
            emit("scalar", lambda e, xt=xt, par=par: e.activation(
                out=gdump, in_=xt, func=AF.Copy, accum_out=sm(f"paS{par}")))
            emit("scalar", lambda e, xt=xt, par=par: e.activation(
                out=gdump, in_=xt, func=AF.Square, accum_out=sm(f"paSS{par}")), "act")
            m0_sa[k] = cnt["act"]

        def m0_b(k):
            par = k % 2
            xt = sx[:, par * D:(par + 1) * D]
            wait("vector", "act", m0_sa[k])
            stats_fin_g(f"paS{par}", f"paSS{par}", sm("q1"), rms=True)
            emit("vector", lambda e, k=k: e.tensor_copy(
                out=sm("m1s", NTC)[:, k:k + 1], in_=sm("mS")))
            emit("vector", lambda e, k=k: e.tensor_copy(
                out=sm("s1s", NTC)[:, k:k + 1], in_=sm("q1")))
            emit("vector", lambda e, xt=xt: e.tensor_scalar(
                out=u32(0, D), in0=xt, scalar1=sm("mS"),
                scalar2=sm("q1"), op0=OP.subtract, op1=OP.mult), "ve")
            x_slot_free[par] = ("ve", cnt["ve"])
            if k + 2 < NTC:
                m0_x(k + 2)
            wait("vector", "mm", xb_slot_free[par])
            emit("vector", lambda e, par=par: e.tensor_tensor(
                out=xb16[:, par * D:(par + 1) * D], in0=u32(0, D), in1=rmsr[:, :],
                op=OP.mult), "ve")
            xb_ready = cnt["ve"]
            vem, mmm = transpose_tile(
                xb16[:, par * D:],
                lambda dc, k=k: bigA[:, dc * TCORE + k * 128: dc * TCORE + k * 128 + 128],
                NKD, pre_wait=("ve", xb_ready))
            xb_slot_free[par] = mmm
            tpve_k[k] = vem

        def mixw_rhs(dc, tb):
            return bigA[:, dc * TCORE + tb * TBS: dc * TCORE + (tb + 1) * TBS]

        def proj_tb(pair, ws, tb, pre_ve):
            seq, wsl, wmark = ws
            wait("tensor", *wmark)
            wait("tensor", "ve", pre_ve)
            pe_take_bank(pair[tb % 2])
            for dc in range(NKD):
                emit("tensor",
                     lambda e, bk=pair[tb % 2], dc=dc, tb=tb, w=wsl:
                     e.matmul(pf[bk][:, :TBS], w[:, dc * 128:(dc + 1) * 128],
                              mixw_rhs(dc, tb), start=(dc == 0),
                              stop=(dc == NKD - 1)),
                     inc="mm" if dc == NKD - 1 else None)
            return cnt["mm"]

        m1_pre = NTB == 2 and NTC >= 8
        if m1_pre:
            pre_m1 = {m: load_slice(wmix[m][0].rearrange("p k n -> p (k n)"),
                                    NKD * 128) for m in ("A", "B", "D")}
        pre_mm = {}
        m0_x(0)
        m0_x(1)
        for k in range(NTC):
            if k >= 1:
                m0_b(k - 1)
            if m1_pre and k in (5, 6, 7):
                m, pr = (("A", (0, 1)), ("B", (2, 3)), ("D", (4, 5)))[k - 5]
                pre_mm[m] = proj_tb(pr, pre_m1[m], 0, tpve_k[3])
            m0_a(k)
        m0_b(NTC - 1)
        if m1_pre:
            for m, pr in (("A", (0, 1)), ("B", (2, 3)), ("D", (4, 5))):
                pre_mm[m] = proj_tb(pr, pre_m1[m], 1, tpve_k[NTC - 1])
                slice_consumed(pre_m1[m][0], "mm", pre_mm[m])
        m0_ve = cnt["ve"]

        A_ = u32(0, TCORE); Uu_ = u32(TCORE, TCORE); DS_ = u32(2 * TCORE, TCORE)
        CS_ = DS_
        H_ = u32(3 * TCORE, TCORE); P_ = u32(4 * TCORE, TCORE)

        def hsl(buf, tb):
            return buf[:, tb * TBS:(tb + 1) * TBS]

        def proj_job(jid, ws, rhs_fn, blklen, nblkm, kchunks):
            seq, wsl, wmark = ws
            pair = (2 * (jid % 3), 2 * (jid % 3) + 1)
            wait("tensor", *wmark)
            for i in range(min(2, nblkm)):
                pe_take_bank(pair[i])
            for dc in range(kchunks):
                for tb in range(nblkm):
                    bk = pair[tb % 2]
                    emit("tensor",
                         lambda e, bk=bk, dc=dc, tb=tb, w=wsl, bl=blklen:
                         e.matmul(pf[bk][:, :bl], w[:, dc * 128:(dc + 1) * 128],
                                  rhs_fn(dc, tb), start=(dc == 0),
                                  stop=(dc == kchunks - 1)),
                         inc="mm" if (tb == nblkm - 1 and dc == kchunks - 1) else None)
            mmv = cnt["mm"]
            slice_consumed(seq, "mm", mmv)
            return pair, mmv

        wait("tensor", "ve", m0_ve)
        jid = 0
        prev_nc_done = m0_ve
        for nnc in range(NNC):
            if nnc == 0 and m1_pre:
                pa, va = (0, 1), pre_mm["A"]
                pb_, vb = (2, 3), pre_mm["B"]
                pd, vd = (4, 5), pre_mm["D"]
                jid = 3
            else:
                wsA = load_slice(wmix["A"][nnc].rearrange("p k n -> p (k n)"),
                                 NKD * 128)
                pa, va = proj_job(jid, wsA, mixw_rhs, TBS, NTB, NKD); jid += 1
                wsB = load_slice(wmix["B"][nnc].rearrange("p k n -> p (k n)"),
                                 NKD * 128)
                pb_, vb = proj_job(jid, wsB, mixw_rhs, TBS, NTB, NKD); jid += 1
                wsD = load_slice(wmix["D"][nnc].rearrange("p k n -> p (k n)"),
                                 NKD * 128)
                pd, vd = proj_job(jid, wsD, mixw_rhs, TBS, NTB, NKD); jid += 1
            wait("scalar", "ve", prev_nc_done)
            wait("scalar", "mm", va)
            for tb in range(NTB):
                emit("scalar", lambda e, b=pa[tb % 2], tb=tb, n=nnc:
                     e.activation(out=hsl(A_, tb), in_=pf[b][:, :TBS], func=AF.Sigmoid,
                                  bias=sAb[:, n:n + 1], scale=1.0),
                     inc="act" if tb == NTB - 1 else None)
            a_done = cnt["act"]
            for b in set(pa[i % 2] for i in range(NTB)):
                consumer_frees(b, "act", a_done)
            wait("scalar", "mm", vd)
            for tb in range(NTB):
                emit("scalar", lambda e, b=pd[tb % 2], tb=tb:
                     e.activation(out=hsl(DS_, tb), in_=pf[b][:, :TBS], func=AF.Sigmoid),
                     inc="act" if tb == NTB - 1 else None)
            ds_done = cnt["act"]
            for b in set(pd[i % 2] for i in range(NTB)):
                consumer_frees(b, "act", ds_done)
            wait("vector", "mm", vb)
            wait("vector", "act", ds_done)
            for tb in range(NTB):
                emit("vector", lambda e, b=pb_[tb % 2], tb=tb:
                     e.tensor_tensor(out=hsl(Uu_, tb), in0=hsl(DS_, tb),
                                     in1=pf[b][:, :TBS], op=OP.mult),
                     inc="ve" if tb == NTB - 1 else None)
            u_done = cnt["ve"]
            for b in set(pb_[i % 2] for i in range(NTB)):
                consumer_frees(b, "ve", u_done)
            wsCa = load_slice(wmix["Ca"][nnc].rearrange("p k n -> p (k n)"), NKD * 128)
            pc, vc = proj_job(jid, wsCa, mixw_rhs, TBS, NTB, NKD); jid += 1
            wsCb = load_slice(wmix["Cb"][nnc].rearrange("p k n -> p (k n)"), NKD * 128)
            pcb, vcb = proj_job(jid, wsCb, mixw_rhs, TBS, NTB, NKD); jid += 1
            wait("vector", "act", a_done)
            emit("vector", lambda e: e.tensor_tensor_scan(
                out=H_, data0=A_, data1=Uu_, initial=0.0, op0=OP.mult, op1=OP.add))
            emit("vector", lambda e: e.tensor_tensor_scan(
                out=P_, data0=A_, data1=zeros[:, 0:TCORE], initial=1.0,
                op0=OP.mult, op1=OP.add))
            emit("vector", lambda e, n=nnc: e.tensor_copy(
                out=sm("hl", NNC)[:, n:n + 1], in_=H_[:, TCORE - 1:TCORE]))
            emit("vector", lambda e, n=nnc: e.tensor_copy(
                out=sm("pt", NNC)[:, n:n + 1], in_=P_[:, TCORE - 1:TCORE]))
            wait("scalar", "ve", u_done)
            wait("scalar", "mm", vc)
            for tb in range(NTB):
                emit("scalar", lambda e, b=pc[tb % 2], tb=tb:
                     e.activation(out=hsl(CS_, tb), in_=pf[b][:, :TBS], func=AF.Silu),
                     inc="act" if tb == NTB - 1 else None)
            cs_done = cnt["act"]
            for b in set(pc[i % 2] for i in range(NTB)):
                consumer_frees(b, "act", cs_done)
            wait("vector", "mm", vcb)
            wait("vector", "act", cs_done)
            for tb in range(NTB):
                emit("vector", lambda e, b=pcb[tb % 2], tb=tb:
                     e.tensor_tensor(out=hsl(CS_, tb), in0=hsl(CS_, tb),
                                     in1=pf[b][:, :TBS], op=OP.mult),
                     inc="ve" if tb == NTB - 1 else None)
            c_done = cnt["ve"]
            for b in set(pcb[i % 2] for i in range(NTB)):
                consumer_frees(b, "ve", c_done)
            emit("vector", lambda e, n=nnc: e.tensor_tensor(
                out=bigB[:, n * TCORE:(n + 1) * TCORE], in0=CS_, in1=H_, op=OP.mult))
            emit("vector", lambda e, n=nnc: e.tensor_tensor(
                out=bigC[:, n * TCORE:(n + 1) * TCORE], in0=CS_, in1=P_, op=OP.mult), "ve")
            prev_nc_done = cnt["ve"]
            if nnc == NNC // 2 - 1:
                wait("sync", "ve", cnt["ve"])
                sdma("exr", "sync", lambda e: e.dma_start(
                    out=ex_in_a[:, 0:NNC // 2], in_=sm("pt", NNC)[:, 0:NNC // 2]))
                exwa = sdma("exr", "sync", lambda e: e.dma_start(
                    out=ex_in_a[:, NNC // 2:NNC], in_=sm("hl", NNC)[:, 0:NNC // 2]))
                wait("gpsimd", *exwa)
                ex_cc_a = cc_issue(lambda e: e.collective_compute(
                    "AllGather", OP.bypass, replica_groups=[list(range(NCORE))],
                    ins=[ex_in_a.opt()], outs=[ex_out_a.opt()]))
        m1_ve = cnt["ve"]

        NNC2 = NNC // 2
        wait("sync", "ve", m1_ve)
        sdma("exr", "sync", lambda e: e.dma_start(
            out=ex_in_b[:, 0:NNC2], in_=sm("pt", NNC)[:, NNC2:NNC]))
        exwb = sdma("exr", "sync", lambda e: e.dma_start(
            out=ex_in_b[:, NNC2:NNC], in_=sm("hl", NNC)[:, NNC2:NNC]))
        wait("gpsimd", *exwb)
        ex_cc_b = cc_issue(lambda e: e.collective_compute(
            "AllGather", OP.bypass, replica_groups=[list(range(NCORE))],
            ins=[ex_in_b.opt()], outs=[ex_out_b.opt()]))
        y_marks = []

        def m2_fold(h, cc_mark, ex_out_h):
            base = h * NCORE * NNC
            off = h * NNC2
            wait("sync", "cc", cc_mark)
            exr = sdma("exr", "sync", lambda e, base=base, ex_out_h=ex_out_h:
                       e.dma_start(
                           out=s_ex[:, base:base + NCORE * NNC].rearrange(
                               "p (c j) -> p c j", c=NCORE),
                           in_=ex_out_h.rearrange("(c p) j -> p c j", p=128)))
            wait("vector", *exr)
            if h == 0:
                emit("vector", lambda e: e.memset(sm("Hin", NNC), 0.0))
                drain("vector")
                emit("vector", lambda e: e.tensor_scalar(
                    out=sm("pm1", NCORE), in0=pm_s[:, :], scalar1=-1.0, scalar2=1.0,
                    op0=OP.mult, op1=OP.add))
                drain("vector")
            for c in range(NCORE):
                emit("vector", lambda e, c=c, base=base: e.tensor_scalar(
                    out=sm("eP", NNC)[:, 0:NNC2],
                    in0=s_ex[:, base + c * NNC: base + c * NNC + NNC2],
                    scalar1=pm_s[:, c:c + 1], scalar2=sm("pm1", NCORE)[:, c:c + 1],
                    op0=OP.mult, op1=OP.add))
                drain("vector")
                emit("vector", lambda e, c=c, base=base: e.tensor_scalar(
                    out=sm("eh", NNC)[:, 0:NNC2],
                    in0=s_ex[:, base + c * NNC + NNC2: base + (c + 1) * NNC],
                    scalar1=pm_s[:, c:c + 1], scalar2=None, op0=OP.mult))
                drain("vector")
                emit("vector", lambda e, off=off: e.tensor_tensor(
                    out=sm("Hin", NNC)[:, off:off + NNC2],
                    in0=sm("Hin", NNC)[:, off:off + NNC2],
                    in1=sm("eP", NNC)[:, 0:NNC2], op=OP.mult))
                drain("vector")
                emit("vector", lambda e, off=off: e.tensor_tensor(
                    out=sm("Hin", NNC)[:, off:off + NNC2],
                    in0=sm("Hin", NNC)[:, off:off + NNC2],
                    in1=sm("eh", NNC)[:, 0:NNC2], op=OP.add))
                drain("vector")
            for nnc in range(off, off + NNC2):
                emit("vector", lambda e, n=nnc: e.scalar_tensor_tensor(
                    out=bigB[:, n * TCORE:(n + 1) * TCORE],
                    in0=bigC[:, n * TCORE:(n + 1) * TCORE],
                    scalar=sm("Hin", NNC)[:, n:n + 1],
                    in1=bigB[:, n * TCORE:(n + 1) * TCORE],
                    op0=OP.mult, op1=OP.add), "ve")
                y_marks.append(cnt["ve"])

        m2_fold(0, ex_cc_a, ex_out_a)
        m2_fold(1, ex_cc_b, ex_out_b)
        y_ready = cnt["ve"]

        def O32(t01):
            return u32(t01 * D, D)
        DUMP = gdump
        base_init = [False]
        spill_mark = {}
        ag1_mark = {}
        m4_tpve = [0]
        wait("tensor", "setupg", setup_g)
        wait("vector", "act", cnt["act"])
        for db in range(ND4):
            for i in range(NNC):
                seq, wsl, wmark = load_slice(wo_in[i][:, db * 512:(db + 1) * 512], 512)
                wait("tensor", *wmark)
                if db == 0:
                    wait("tensor", "ve", y_marks[i])
                for tc in range(NTC):
                    if i == 0:
                        pe_take_bank(tc % 8)
                    emit("tensor",
                         lambda e, bk=tc % 8, i=i, tc=tc, w=wsl:
                         e.matmul(pf[bk][:, :],
                                  bigB[:, i * TCORE + tc * 128: i * TCORE + tc * 128 + 128],
                                  w[:, 0:512],
                                  start=(i == 0), stop=(i == NNC - 1)),
                         inc="mm" if tc == NTC - 1 else None)
                slice_consumed(seq, "mm", cnt["mm"])
            mmdb = cnt["mm"]
            wait("vector", "mm", mmdb)
            for tc in range(NTC):
                emit("vector", lambda e, bk=tc % 8, tc=tc, db=db: e.tensor_tensor(
                    out=bigC[:, tc * D + db * 512: tc * D + db * 512 + 512],
                    in0=pf[bk][:, :], in1=outbr[:, db * 512:(db + 1) * 512],
                    op=OP.add), "ve")
                consumer_frees(tc % 8, "ve", cnt["ve"])
        oproj_ve = cnt["ve"]
        xmarks = {}
        sa_mark = {}
        sb_mark = {}

        def stats_fin(par, dst):
            emit("vector", lambda e, par=par: e.tensor_scalar(
                out=sm("mS"), in0=sm(f"paS{par}"), scalar1=1.0 / D, scalar2=None,
                op0=OP.mult))
            emit("vector", lambda e, par=par: e.tensor_scalar(
                out=sm("eS"), in0=sm(f"paSS{par}"), scalar1=1.0 / D, scalar2=None,
                op0=OP.mult))
            emit("vector", lambda e: e.tensor_tensor(out=sm("m2S"), in0=sm("mS"),
                                                     in1=sm("mS"), op=OP.mult))
            emit("vector", lambda e: e.tensor_tensor(out=sm("vS"), in0=sm("eS"),
                                                     in1=sm("m2S"), op=OP.subtract))
            emit("vector", lambda e: e.tensor_scalar(out=sm("qS"), in0=sm("vS"),
                                                     scalar1=1.0, scalar2=EPS,
                                                     op0=OP.mult, op1=OP.add))
            newton_rsqrt(dst, sm("qS"))

        def m4_x(k):
            par = k % 2
            wait("sync", *x_slot_free[par])
            xmarks[k] = sdma(f"x{par}", "sync",
                             lambda e, k=k, par=par: e.dma_start(
                                 out=sx[:, par * D:(par + 1) * D],
                                 in_=x_in[k * 128:(k + 1) * 128, :]))

        def m4_a(k):
            par = k % 2
            o = u32(par * D, D)
            wait("vector", *xmarks[k])
            if k >= 2:
                wait("vector", *spill_mark[k - 2])
            emit("vector", lambda e, k=k, par=par: e.tensor_scalar(
                out=DUMP, in0=sx[:, par * D:(par + 1) * D],
                scalar1=sm("m1s", NTC)[:, k:k + 1],
                scalar2=sm("s1s", NTC)[:, k:k + 1],
                op0=OP.subtract, op1=OP.mult))
            emit("vector", lambda e, o=o, k=k: e.tensor_tensor(
                out=o, in0=bigC[:, k * D:(k + 1) * D], in1=DUMP, op=OP.add), "ve")
            wait("scalar", "ve", cnt["ve"])
            emit("scalar", lambda e, o=o, par=par: e.activation(
                out=DUMP, in_=o, func=AF.Copy, accum_out=sm(f"paS{par}")))
            emit("scalar", lambda e, o=o, par=par: e.activation(
                out=DUMP, in_=o, func=AF.Square, accum_out=sm(f"paSS{par}")), "act")
            sa_mark[k] = cnt["act"]

        def m4_b(k):
            par = k % 2
            o = u32(par * D, D)
            wait("vector", "act", sa_mark[k])
            stats_fin(par, sm("s2"))
            emit("vector", lambda e, o=o: e.tensor_scalar(
                out=o, in0=o, scalar1=sm("mS"), scalar2=sm("s2"),
                op0=OP.subtract, op1=OP.mult))
            emit("vector", lambda e, o=o, par=par: e.tensor_tensor(
                out=o, in0=o, in1=sx[:, par * D:(par + 1) * D], op=OP.add), "ve")
            xr_done = cnt["ve"]
            wait("sync", "ve", xr_done)
            spill_mark[k] = sdma("spl", "sync",
                                 lambda e, o=o, k=k: e.dma_start(
                                     out=xr1_sp[k * 128:(k + 1) * 128, :], in_=o))
            wait("scalar", "ve", xr_done)
            emit("scalar", lambda e, o=o, par=par: e.activation(
                out=sx[:, par * D:(par + 1) * D], in_=o, func=AF.Copy,
                accum_out=sm(f"pbS{par}")))
            emit("scalar", lambda e, o=o, par=par: e.activation(
                out=sx[:, par * D:(par + 1) * D], in_=o, func=AF.Square,
                accum_out=sm(f"pbSS{par}")), "act")
            sb_mark[k] = cnt["act"]
            x_slot_free[par] = ("act", sb_mark[k])
            if k + 2 < NTC:
                m4_x(k + 2)

        def stats_fin_b(par, dst):
            emit("vector", lambda e, par=par: e.tensor_scalar(
                out=sm("mS"), in0=sm(f"pbS{par}"), scalar1=1.0 / D, scalar2=None,
                op0=OP.mult))
            emit("vector", lambda e, par=par: e.tensor_scalar(
                out=sm("eS"), in0=sm(f"pbSS{par}"), scalar1=1.0 / D, scalar2=None,
                op0=OP.mult))
            emit("vector", lambda e: e.tensor_tensor(out=sm("m2S"), in0=sm("mS"),
                                                     in1=sm("mS"), op=OP.mult))
            emit("vector", lambda e: e.tensor_tensor(out=sm("vS"), in0=sm("eS"),
                                                     in1=sm("m2S"), op=OP.subtract))
            emit("vector", lambda e: e.tensor_scalar(out=sm("qS"), in0=sm("vS"),
                                                     scalar1=1.0, scalar2=EPS,
                                                     op0=OP.mult, op1=OP.add))
            newton_rsqrt(dst, sm("qS"))

        def m4_c(k):
            par = k % 2
            o = u32(par * D, D)
            wait("vector", "act", sb_mark[k])
            stats_fin_b(par, sm("s2"))
            if k >= 2:
                wait("vector", *ag1_mark[k - 2])
            wait("vector", "mm", xb_slot_free[par])
            emit("vector", lambda e, o=o, par=par: e.tensor_scalar(
                out=xb16[:, par * D:(par + 1) * D], in0=o, scalar1=sm("mS"),
                scalar2=sm("s2"), op0=OP.subtract, op1=OP.mult), "ve")
            xffn_done = cnt["ve"]
            wait("sync", "ve", xffn_done)
            ag1_mark[k] = sdma("ag1s", "sync",
                               lambda e, k=k, par=par: e.dma_start(
                                   out=ag1_in[k * 128:(k + 1) * 128, :],
                                   in_=xb16[:, par * D:(par + 1) * D]))
            if k == NTC // 2 - 1:
                wait("gpsimd", *ag1_mark[k])
                cc_issue(lambda e: e.collective_compute(
                    "AllGather", OP.bypass, replica_groups=[list(range(NCORE))],
                    ins=[ag1_in[0:TCORE // 2, :].opt()],
                    outs=[ag1_out[0:NCORE * (TCORE // 2), :].opt()]))
            tve, tmm = transpose_tile(
                xb16[:, par * D:],
                lambda dc, k=k: bigA[:, dc * TCORE + k * 128:
                                     dc * TCORE + k * 128 + 128],
                NKD, pre_wait=("ve", xffn_done))
            xb_slot_free[par] = tmm
            m4_tpve[0] = tve
            pe_take_bank(5)
            wait("tensor", "ve", tve)
            for dc in range(NKD):
                emit("tensor", lambda e, dc=dc, k=k: e.matmul(
                    pf[5][:, 0:E],
                    bigA[:, dc * TCORE + k * 128: dc * TCORE + k * 128 + 128],
                    sgateb[:, dc * E:(dc + 1) * E],
                    start=(dc == 0), stop=(dc == NKD - 1)),
                    inc="mm" if dc == NKD - 1 else None)
            wait("vector", "mm", cnt["mm"])
            emit("vector", lambda e, k=k: e.tensor_copy(
                out=sm("lgs", NTC * E)[:, k * E:(k + 1) * E], in_=pf[5][:, 0:E]), "ve")
            consumer_frees(5, "ve", cnt["ve"])

        m4_x(0)
        m4_x(1)
        for k in range(NTC):
            if k >= 2:
                m4_c(k - 2)
            if k >= 1:
                m4_b(k - 1)
            m4_a(k)
        m4_b(NTC - 1)
        m4_c(NTC - 2)
        m4_c(NTC - 1)
        rout_ve_h = [0]
        oh_mark = {}

        def routing_part1(tc):
            r = 8 * tc
            lgt = sm("lgs", NTC * E)[:, tc * E:(tc + 1) * E]
            emit("vector", lambda e, lgt=lgt: e.reduce_max(out=sm("rmx"), in_=lgt,
                                                           axis=AX))
            emit("vector", lambda e, lgt=lgt: e.tensor_scalar(
                out=sm("zc", E), in0=lgt, scalar1=sm("rmx"), scalar2=None,
                op0=OP.subtract), "ve")
            wait("scalar", "ve", cnt["ve"])
            emit("scalar", lambda e: e.activation(out=sm("sg", E), in_=sm("zc", E),
                                                  func=AF.Sigmoid), "act")
            wait("vector", "act", cnt["act"])
            emit("vector", lambda e: e.tensor_scalar(out=sm("om", E), in0=sm("sg", E),
                                                     scalar1=-1.0, scalar2=1.0,
                                                     op0=OP.mult, op1=OP.add))
            emit("vector", lambda e: e.reciprocal(out=sm("om", E), in_=sm("om", E)))
            emit("vector", lambda e: e.tensor_tensor(out=sm("esc", E), in0=sm("sg", E),
                                                     in1=sm("om", E), op=OP.mult))
            emit("vector", lambda e: e.reduce_sum(out=sm("ssum"), in_=sm("esc", E),
                                                  axis=AX))
            emit("vector", lambda e: e.reciprocal(out=sm("ssum"), in_=sm("ssum")))
            emit("vector", lambda e: e.tensor_scalar(out=sm("esc", E),
                                                     in0=sm("esc", E),
                                                     scalar1=sm("ssum"), scalar2=None,
                                                     op0=OP.mult))
            emit("vector", lambda e: e.max(out=sm("mx8", 8), in_=sm("esc", E)))
            emit("vector", lambda e: e.max_index(
                out=sm("mi8", 8).bitcast(U32), in_max=sm("mx8", 8),
                in_values=sm("esc", E)))
            emit("vector", lambda e, r=r: e.tensor_copy(out=s_rout[:, r + 2:r + 4],
                                                        in_=sm("mx8", 8)[:, 0:2]))
            emit("vector", lambda e, r=r: e.tensor_copy(
                out=s_rout[:, r:r + 2], in_=sm("mi8", 8).bitcast(U32)[:, 0:2]))
            emit("vector", lambda e, r=r: e.tensor_scalar(
                out=sm("m1e", E), in0=siota[:, :], scalar1=s_rout[:, r:r + 1],
                scalar2=None, op0=OP.is_equal))
            emit("vector", lambda e, r=r: e.tensor_scalar(
                out=sm("m2e", E), in0=siota[:, :], scalar1=s_rout[:, r + 1:r + 2],
                scalar2=None, op0=OP.is_equal))
            emit("vector", lambda e: e.tensor_tensor(
                out=sm("ohb", 8).bitcast(BF16)[:, 0:E], in0=sm("m1e", E),
                in1=sm("m2e", E), op=OP.add), "ve")
            oh_mark[tc] = cnt["ve"]

        def routing_part2(tc):
            r = 8 * tc
            wait("tensor", "ve", oh_mark[tc])
            pe_take_bank(6)
            emit("tensor", lambda e: e.matmul(pf[6][:, 0:E], slt[:, :],
                                              sm("ohb", 8).bitcast(BF16)[:, 0:E],
                                              start=True, stop=True), "mm")
            pe_take_bank(7)
            emit("tensor", lambda e: e.matmul(pf[7][:, 0:E], sones[:, :],
                                              sm("ohb", 8).bitcast(BF16)[:, 0:E],
                                              start=True, stop=True), "mm")
            pos_mm = cnt["mm"]
            wait("vector", "mm", pos_mm)
            if not base_init[0]:
                emit("vector", lambda e: e.memset(sm("brep", E), 0.0))
                base_init[0] = True
            emit("vector", lambda e: e.tensor_tensor(out=sm("pos", E),
                                                     in0=pf[6][:, 0:E],
                                                     in1=sm("brep", E), op=OP.add),
                 "ve")
            consumer_frees(6, "ve", cnt["ve"])
            emit("vector", lambda e: e.tensor_tensor(out=sm("brep", E),
                                                     in0=sm("brep", E),
                                                     in1=pf[7][:, 0:E], op=OP.add),
                 "ve")
            consumer_frees(7, "ve", cnt["ve"])
            emit("vector", lambda e, r=r: e.scalar_tensor_tensor(
                out=sm("tmpe", E), in0=sm("pos", E), scalar=1.0, in1=sm("m1e", E),
                op0=OP.mult, op1=OP.mult, accum_out=s_rout[:, r + 4:r + 5]))
            emit("vector", lambda e, r=r: e.scalar_tensor_tensor(
                out=sm("tmpe", E), in0=sm("pos", E), scalar=1.0, in1=sm("m2e", E),
                op0=OP.mult, op1=OP.mult, accum_out=s_rout[:, r + 5:r + 6]), "ve")
            if tc == NTC - 1:
                rout_ve_h[0] = cnt["ve"]

        NP = 2 * NTC

        def pairs_ap(base, extra_e=False):
            a = s_rout[:, base:]
            dims = [list(a.ap[0]), [8, NTC], [1, 2]] + ([[0, E]] if extra_e else [])
            return bass.AP(a.tensor, a.offset, dims, None)

        m4_act = cnt["act"]

        def m5_counts():
            wait("sync", "ve", rout_ve_h[0])
            cntw = sdma("exr", "sync",
                        lambda e: e.dma_start(out=cnt_in, in_=sm("brep", E)[0:1, :]))
            wait("gpsimd", *cntw)
            cc_issue(lambda e: e.collective_compute(
                "AllGather", OP.bypass, replica_groups=[list(range(NCORE))],
                ins=[cnt_in.opt()], outs=[cnt_out.opt()]))
            wait("sync", "cc", cnt["cc"])
            cntr = sdma("exr", "sync", lambda e: e.dma_start(
                out=scnt[:, :], in_=cnt_out.rearrange("c e -> (c e)").unsqueeze(0)))
            wait("vector", *cntr)
            emit("vector", lambda e: e.memset(sm("cb", E)[0:1, :], 0.0))
            for c in range(NCORE):
                emit("vector", lambda e, c=c: e.scalar_tensor_tensor(
                    out=sm("cb", E)[0:1, :], in0=scnt[0:1, c * E:(c + 1) * E],
                    scalar=apm_s[0:1, c:c + 1], in1=sm("cb", E)[0:1, :],
                    op0=OP.mult, op1=OP.add))
            emit("vector", lambda e: e.memset(sm("o128", 128)[0:1, :], 1.0), "ve")
            return ("ve", cnt["ve"])

        def m5_cbr(cb_mark):
            wait("tensor", *cb_mark)
            pe_take_bank(4)
            emit("tensor", lambda e: e.matmul(pf[4][:, 0:E], sm("o128", 128)[0:1, :],
                                              sm("cb", E)[0:1, :], start=True, stop=True),
                 "mm")
            cbr_mm = cnt["mm"]
            wait("scalar", "mm", cbr_mm)
            emit("scalar", lambda e: e.copy(out=sm("cbr", E), in_=pf[4][:, 0:E]), "act")
            consumer_frees(4, "act", cnt["act"])
            return cnt["act"]

        def m5_batch(cbr_act):
            mall = g_bf.ap().bitcast(F32)[:, 0:NP * E]
            tA = g_bf.ap().bitcast(F32)[:, NP * E:NP * E + NP]
            tB = g_bf.ap().bitcast(F32)[:, NP * E + NP:NP * E + 2 * NP]
            tCi = g_bf.ap().bitcast(I32)[:, NP * E + 2 * NP:NP * E + 3 * NP]
            tDi = g_bf.ap().bitcast(I32)[:, NP * E + 3 * NP:NP * E + 4 * NP]
            g = "vector"
            wait(g, "act", cbr_act)
            wait(g, "act", m4_act)
            for pr in range(NP):
                tc, j = divmod(pr, 2)
                emit(g, lambda e, tc=tc, j=j: e.tensor_scalar(
                    out=mall[:, 0:E], in0=siota[:, :],
                    scalar1=s_rout[:, 8 * tc + j:8 * tc + j + 1],
                    scalar2=None, op0=OP.is_equal))
                emit(g, lambda e, pr=pr: e.scalar_tensor_tensor(
                    out=mall[:, E:2 * E], in0=sm("cbr", E), scalar=1.0,
                    in1=mall[:, 0:E], op0=OP.mult, op1=OP.mult,
                    accum_out=bass.AP(tA.tensor, tA.offset + pr,
                                      [list(tA.ap[0]), [1, 1]], None)))
            emit(g, lambda e: e.tensor_tensor(out=pairs_ap(4), in0=pairs_ap(4),
                                              in1=tA, op=OP.add))
            emit(g, lambda e: e.tensor_scalar(out=tA, in0=pairs_ap(0),
                                              scalar1=float(CAP), scalar2=None,
                                              op0=OP.mult))
            emit(g, lambda e: e.tensor_tensor(out=tB, in0=tA, in1=pairs_ap(4),
                                              op=OP.add))
            emit(g, lambda e: e.tensor_scalar(out=mall[:, 0:NP], in0=pairs_ap(4),
                                              scalar1=float(CAP) - 0.5, scalar2=None,
                                              op0=OP.is_gt))
            emit(g, lambda e: e.scalar_tensor_tensor(
                out=tB, in0=mall[:, 0:NP], scalar=float(2 * E * CAP), in1=tB,
                op0=OP.mult, op1=OP.add))
            emit(g, lambda e: e.tensor_copy(out=s_scat[:, :], in_=tB))
            emit(g, lambda e: e.tensor_copy(out=tCi, in_=pairs_ap(0)))
            emit(g, lambda e: e.tensor_scalar(out=tDi, in0=tCi, scalar1=1, scalar2=None,
                                              op0=OP.bitwise_and))
            emit(g, lambda e: e.tensor_scalar(out=tCi, in0=tCi, scalar1=1, scalar2=None,
                                              op0=OP.logical_shift_right))
            emit(g, lambda e: e.tensor_scalar(out=tCi, in0=tCi, scalar1=CAP, scalar2=None,
                                              op0=OP.mult))
            emit(g, lambda e: e.tensor_scalar(out=tDi, in0=tDi, scalar1=E * CAP // 2,
                                              scalar2=None, op0=OP.mult))
            emit(g, lambda e: e.tensor_tensor(out=tCi, in0=tCi, in1=tDi, op=OP.add))
            emit(g, lambda e: e.tensor_scalar(out=tB, in0=pairs_ap(4),
                                              scalar1=float(CAP - 1), scalar2=None,
                                              op0=OP.min))
            emit(g, lambda e: e.tensor_copy(out=tDi, in_=tB))
            emit(g, lambda e: e.tensor_tensor(out=s_gidx[:, :], in0=tCi, in1=tDi,
                                              op=OP.add))
            emit(g, lambda e: e.tensor_scalar(out=s_gidx2[:, :], in0=s_gidx[:, :],
                                              scalar1=E * CAP, scalar2=None,
                                              op0=OP.add), "ve")
            return cnt["ve"]

        def m5_scatters(batch_ve):
            wait("gpsimd", "ve", batch_ve)
            mzw = sdma("mz", "gpsimd",
                       lambda e: e.dma_start(out=map_in_t[0:MROW, :],
                                             in_=zeros[0:MROW, 0:128]))
            wait("gpsimd", *mzw)
            for tc in range(NTC):
                for j in (0, 1):
                    emit("gpsimd", lambda e, tc=tc, j=j: e.indirect_dma_start(
                        out=map_in_t.rearrange("a b -> (a b)").unsqueeze(1),
                        out_offset=IndirectOffsetOnAxis(
                            ap=s_scat[:, 2 * tc + j:2 * tc + j + 1], axis=0),
                        in_=s_sval[:, tc:tc + 1], in_offset=None), "mz", 16)
            wait("gpsimd", "mz", cnt["mz"])
            mcc = cc_issue(lambda e: e.collective_compute(
                "AllReduce", OP.add, replica_groups=[list(range(NCORE))],
                ins=[map_in_t[0:MROW, :].opt()], outs=[map_out_t.opt()]))
            return mcc

        def e_map_post(mcc):
            wait("gpsimd", "cc", mcc)
            wait("gpsimd", "ve", m16_done)
            mlw = sdma("mz", "gpsimd", lambda e: e.indirect_dma_start(
                out=s_map16[:, :], out_offset=None,
                in_=map_out_t, in_offset=IndirectOffsetOnAxis(ap=s_m16[:, :], axis=0)))
            wait("tensor", *mlw)
            pe_take_bank(4)
            emit("tensor", lambda e: e.matmul(
                pf[4][:, 0:EPC * NSC], s_map16[:, :], sidnf[0:EPC * NSC, 0:EPC * NSC],
                start=True, stop=True), "mm")
            wait("scalar", "mm", cnt["mm"])
            emit("scalar", lambda e: e.copy(out=s_mapT[:, :], in_=pf[4][:, 0:EPC * NSC]),
                 "act")
            consumer_frees(4, "act", cnt["act"])
            mt = cnt["act"]
            wait("vector", "act", mt)
            emit("vector", lambda e: e.tensor_scalar(out=s_mapT[:, :], in0=s_mapT[:, :],
                                                     scalar1=-1.0, scalar2=0.0,
                                                     op0=OP.add, op1=OP.max))
            emit("vector", lambda e: e.tensor_copy(out=s_migx[:, :], in_=s_mapT[:, :]))
            HALF = TCORE // 2
            LH = HALF.bit_length() - 1
            LT = TCORE.bit_length() - 1
            W16 = EPC * NSC
            tA = u32(TCORE, W16).bitcast(I32)
            tB = u32(TCORE + W16, W16).bitcast(I32)
            emit("vector", lambda e: e.tensor_scalar(out=tA, in0=s_migx[:, :],
                                                     scalar1=LH, scalar2=None,
                                                     op0=OP.logical_shift_right))
            emit("vector", lambda e: e.tensor_scalar(out=tA, in0=tA, scalar1=1,
                                                     scalar2=None,
                                                     op0=OP.bitwise_and))
            emit("vector", lambda e: e.tensor_scalar(out=tA, in0=tA,
                                                     scalar1=NCORE * HALF,
                                                     scalar2=None, op0=OP.mult))
            emit("vector", lambda e: e.tensor_scalar(out=tB, in0=s_migx[:, :],
                                                     scalar1=LT, scalar2=None,
                                                     op0=OP.logical_shift_right))
            emit("vector", lambda e: e.tensor_scalar(out=tB, in0=tB, scalar1=HALF,
                                                     scalar2=None, op0=OP.mult))
            emit("vector", lambda e: e.tensor_tensor(out=tA, in0=tA, in1=tB,
                                                     op=OP.add))
            emit("vector", lambda e: e.tensor_scalar(out=tB, in0=s_migx[:, :],
                                                     scalar1=HALF - 1, scalar2=None,
                                                     op0=OP.bitwise_and))
            emit("vector", lambda e: e.tensor_tensor(out=s_migx[:, :], in0=tA, in1=tB,
                                                     op=OP.add), "ve")
            return ("ve", cnt["ve"])

        def e_dispatch_chunk(el, sc, ag1cc, migx):
            slot = sc % 2
            wait("gpsimd", "cc", ag1cc)
            wait("gpsimd", *migx)
            wait("gpsimd", "mm", xb_slot_free[slot])
            gmark = sdma("mz", "gpsimd",
                         lambda e, el=el, sc=sc, s=slot: e.indirect_dma_start(
                             out=xb16[:, s * D:(s + 1) * D], out_offset=None,
                             in_=ag1_out,
                             in_offset=IndirectOffsetOnAxis(
                                 ap=s_migx[:, el * NSC + sc:el * NSC + sc + 1],
                                 axis=0)))
            tve, tmm = transpose_tile(
                xb16[:, slot * D:],
                lambda dc, sc=sc: bigC[:, dc * CAP + sc * 128: dc * CAP + sc * 128 + 128],
                NKD, pre_wait=gmark)
            xb_slot_free[slot] = tmm
            return tve

        wait("tensor", "ve", m4_tpve[0])
        wait("scalar", "spl", cnt["spl"])
        jid_s = 0
        prev_gu_done = cnt["ve"]
        m5_state = {}
        disp0_ve = 0
        disp_cnt = [0]
        if NKIS >= 10:
            ic_counts, ic_cbr, ic_scat, ic_mappost = NTC + 1, NTC + 4, NTC + 5, NTC + 8
        else:
            ic_counts, ic_cbr, ic_scat, ic_mappost = NTC, NTC + 1, NTC + 1, NTC + 1
        for ic in range(NKIS):
            wsgs = load_slice(wsg_in[ic].rearrange("p k n -> p (k n)"), NKD * 128)
            gp, gv = proj_job(jid_s, wsgs,
                              lambda dc, tb: bigA[:, dc * TCORE + tb * TBS:
                                                  dc * TCORE + (tb + 1) * TBS],
                              TBS, NTB, NKD)
            jid_s += 1
            wsus = load_slice(wsu_in[ic].rearrange("p k n -> p (k n)"), NKD * 128)
            up, uv = proj_job(jid_s, wsus,
                              lambda dc, tb: bigA[:, dc * TCORE + tb * TBS:
                                                  dc * TCORE + (tb + 1) * TBS],
                              TBS, NTB, NKD)
            jid_s += 1
            wait("scalar", "ve", prev_gu_done)
            wait("scalar", "mm", gv)
            for tb in range(NTB):
                emit("scalar", lambda e, b=gp[tb % 2], tb=tb:
                     e.activation(out=hsl(A_, tb), in_=pf[b][:, :TBS], func=AF.Silu),
                     inc="act" if tb == NTB - 1 else None)
            gs_done = cnt["act"]
            for b in set(gp[i % 2] for i in range(NTB)):
                consumer_frees(b, "act", gs_done)
            wait("vector", "mm", uv)
            wait("vector", "act", gs_done)
            for tb in range(NTB):
                emit("vector", lambda e, b=up[tb % 2], tb=tb, ic=ic:
                     e.tensor_tensor(
                         out=bigB[:, ic * TCORE + tb * TBS: ic * TCORE + (tb + 1) * TBS],
                         in0=hsl(A_, tb), in1=pf[b][:, :TBS], op=OP.mult),
                     inc="ve" if tb == NTB - 1 else None)
            gu_done = cnt["ve"]
            prev_gu_done = gu_done
            for b in set(up[i % 2] for i in range(NTB)):
                consumer_frees(b, "ve", gu_done)
            if ic == 0:
                wait("gpsimd", *ag1_mark[NTC - 1])
                m5_state["acc"] = cc_issue(lambda e: e.collective_compute(
                    "AllGather", OP.bypass, replica_groups=[list(range(NCORE))],
                    ins=[ag1_in[TCORE // 2:TCORE, :].opt()],
                    outs=[ag1_out[NCORE * (TCORE // 2):, :].opt()]))
            if 1 <= ic <= NTC:
                routing_part2(ic - 1)
            if ic < NTC:
                routing_part1(ic)
            if ic == ic_counts:
                m5_state["cb"] = m5_counts()
            if ic == ic_cbr:
                m5_state["cbr"] = m5_cbr(m5_state["cb"])
                m5_state["bve"] = m5_batch(m5_state["cbr"])
            if ic == ic_scat:
                m5_state["mcc"] = m5_scatters(m5_state["bve"])
            if ic == ic_mappost:
                m5_state["migx"] = e_map_post(m5_state["mcc"])
            if ic >= ic_mappost and disp_cnt[0] < NSC:
                disp0_ve = e_dispatch_chunk(0, disp_cnt[0],
                                            m5_state["acc"], m5_state["migx"])
                disp_cnt[0] += 1
        while disp_cnt[0] < NSC:
            disp0_ve = e_dispatch_chunk(0, disp_cnt[0],
                                        m5_state["acc"], m5_state["migx"])
            disp_cnt[0] += 1
        shared_gu_ve = cnt["ve"]

        eo_last = ("ag2s", 0)
        eo_marks = {}
        eo_ctr = [0]
        NGSL = (2 * D) // 512
        ag2cc = {}
        for el in range(EPC):
            if el > 0:
                bufT_ve = disp1_ve[-1] if disp1_ve else cnt["ve"]
            else:
                bufT_ve = disp0_ve
            wait("tensor", "ve", bufT_ve)
            jid_e = 0
            prev_gue = cnt["ve"]
            for icx in range(NKI):
                wge = load_slice(wg_in[el][icx].rearrange("p k n -> p (k n)"), NKD * 128)
                gp, gv2 = proj_job(jid_e, wge,
                                   lambda dc, tb: bigC[:, dc * CAP + tb * SBS:
                                                       dc * CAP + (tb + 1) * SBS],
                                   SBS, NSB, NKD)
                jid_e += 1
                wue = load_slice(wu_in[el][icx].rearrange("p k n -> p (k n)"), NKD * 128)
                up, uv2 = proj_job(jid_e, wue,
                                   lambda dc, tb: bigC[:, dc * CAP + tb * SBS:
                                                       dc * CAP + (tb + 1) * SBS],
                                   SBS, NSB, NKD)
                jid_e += 1
                wait("scalar", "ve", prev_gue)
                wait("scalar", "mm", gv2)
                for tb in range(NSB):
                    emit("scalar", lambda e, b=gp[tb % 2], tb=tb:
                         e.activation(out=U[:, tb * SBS:(tb + 1) * SBS],
                                      in_=pf[b][:, :SBS], func=AF.Silu),
                         inc="act" if tb == NSB - 1 else None)
                gs_done = cnt["act"]
                for b in set(gp[i % 2] for i in range(NSB)):
                    consumer_frees(b, "act", gs_done)
                wait("vector", "mm", uv2)
                wait("vector", "act", gs_done)
                for tb in range(NSB):
                    emit("vector", lambda e, b=up[tb % 2], tb=tb, icx=icx:
                         e.tensor_tensor(
                             out=bigA[:, icx * CAP + tb * SBS: icx * CAP + (tb + 1) * SBS],
                             in0=U[:, tb * SBS:(tb + 1) * SBS], in1=pf[b][:, :SBS],
                             op=OP.mult),
                         inc="ve" if tb == NSB - 1 else None)
                gu_done = cnt["ve"]
                prev_gue = gu_done
                for b in set(up[i % 2] for i in range(NSB)):
                    consumer_frees(b, "ve", gu_done)
            gue_ve = cnt["ve"]
            wait("tensor", "ve", gue_ve)
            wait("vector", "act", cnt["act"])
            disp1_ve = []
            for db in range(ND4):
                for i in range(NKI):
                    seq, wsl, wmark = load_slice(
                        wd_in[el][i][:, db * 512:(db + 1) * 512], 512)
                    wait("tensor", *wmark)
                    for sc in range(NSC):
                        if i == 0:
                            pe_take_bank(sc % 8)
                        emit("tensor",
                             lambda e, bk=sc % 8, i=i, sc=sc, w=wsl:
                             e.matmul(pf[bk][:, :],
                                      bigA[:, i * CAP + sc * 128: i * CAP + sc * 128 + 128],
                                      w[:, 0:512],
                                      start=(i == 0), stop=(i == NKI - 1)),
                             inc="mm" if sc == NSC - 1 else None)
                    slice_consumed(seq, "mm", cnt["mm"])
                mmdb = cnt["mm"]
                wait("vector", "mm", mmdb)
                for sc in range(NSC):
                    gsl = eo_ctr[0] % NGSL
                    eo_ctr[0] += 1
                    if eo_marks.get(gsl):
                        wait("vector", *eo_marks[gsl])
                    emit("vector", lambda e, bk=sc % 8, gsl=gsl: e.tensor_copy(
                        out=g_bf[:, gsl * 512:(gsl + 1) * 512], in_=pf[bk][:, :]),
                        inc="ve")
                    consumer_frees(sc % 8, "ve", cnt["ve"])
                    wait("sync", "ve", cnt["ve"])
                    hh = db // (ND4 // NH)
                    cb_ = db % (ND4 // NH)
                    eo_marks[gsl] = sdma("ag2s", "sync",
                                         lambda e, el=el, sc=sc, hh=hh, cb_=cb_,
                                         gsl=gsl:
                                         e.dma_start(
                                             out=ag2_in[hh * EPC * CAP + el * CAP
                                                        + sc * 128:
                                                        hh * EPC * CAP + el * CAP
                                                        + sc * 128 + 128,
                                                        cb_ * 512:(cb_ + 1) * 512],
                                             in_=g_bf[:, gsl * 512:(gsl + 1) * 512]))
                    eo_last = eo_marks[gsl]
                if el == 0:
                    for k in range(2):
                        scn = db * 2 + k
                        if scn < NSC:
                            disp1_ve.append(e_dispatch_chunk(
                                1, scn, m5_state["acc"], m5_state["migx"]))
                if (db + 1) % (ND4 // NH) == 0:
                    hh = db // (ND4 // NH)
                    wait("gpsimd", *eo_last)
                    ag2cc[(el, hh)] = cc_issue(
                        lambda e, el=el, hh=hh: e.collective_compute(
                            "AllGather", OP.bypass,
                            replica_groups=[list(range(NCORE))],
                            ins=[ag2_in[hh * EPC * CAP + el * CAP:
                                        hh * EPC * CAP + (el + 1) * CAP, :].opt()],
                            outs=[ag2_out[(hh * EPC + el) * NCORE * CAP:
                                          (hh * EPC + el + 1) * NCORE * CAP,
                                          :].opt()]))
        ag2_cc = ag2cc[(EPC - 1, NH - 1)]

        out_mark = {}
        fa_mark = {}
        gm_marks = {}
        stats_mark = {}
        xrm_mark = {}
        wait("tensor", "ve", shared_gu_ve)
        wait("vector", "act", cnt["act"])
        wait("vector", "spl", cnt["spl"])
        fdump = bigA.ap().bitcast(F32)[:, 0:D]

        def f_pair(k):
            j = k % 3
            if j == 0:
                return g_bf.ap()[:, 0:D], g_bf.ap()[:, D:2 * D]
            if j == 1:
                return xb16.ap()[:, 0:D], xb16.ap()[:, D:2 * D]
            return (zeros.ap().bitcast(BF16)[:, 0:D],
                    U[:, 2 * D:].bitcast(BF16)[:, 0:D])

        def f_gather(k):
            g0, g1 = f_pair(k)
            if k < 3:
                wait("gpsimd", "cc", ag2_cc)
            else:
                wait("gpsimd", *fa_mark[k - 3])
            for hh in range(NH):
                gx = s_gidx if hh == 0 else s_gidx2
                m0 = sdma("mz", "gpsimd",
                          lambda e, k=k, g0=g0, hh=hh, gx=gx: e.indirect_dma_start(
                              out=g0[:, hh * DH:(hh + 1) * DH], out_offset=None,
                              in_=ag2_out,
                              in_offset=IndirectOffsetOnAxis(
                                  ap=gx[:, 2 * k:2 * k + 1], axis=0)))
                m1 = sdma("mz2", "gpsimd",
                          lambda e, k=k, g1=g1, hh=hh, gx=gx: e.indirect_dma_start(
                              out=g1[:, hh * DH:(hh + 1) * DH], out_offset=None,
                              in_=ag2_out,
                              in_offset=IndirectOffsetOnAxis(
                                  ap=gx[:, 2 * k + 1:2 * k + 2], axis=0)))
            gm_marks[k] = (m0, m1)

        fb_apply_ve = {}

        def f_fa(k):
            par = k % 2
            Y = u32(par * D, D)
            r = 8 * k
            g0, g1 = f_pair(k)
            m0, m1 = gm_marks[k]
            wait("vector", *m0)
            wait("vector", *m1)
            emit("vector", lambda e, r=r, Y=Y, g0=g0, k=k: e.scalar_tensor_tensor(
                out=Y, in0=g0, scalar=s_rout[:, r + 2:r + 3],
                in1=bigC[:, k * D:(k + 1) * D], op0=OP.mult, op1=OP.add))
            emit("vector", lambda e, r=r, Y=Y, g1=g1: e.scalar_tensor_tensor(
                out=Y, in0=g1, scalar=s_rout[:, r + 3:r + 4],
                in1=Y, op0=OP.mult, op1=OP.add), "ve")
            fa_mark[k] = ("ve", cnt["ve"])
            if k + 3 < NTC:
                f_gather(k + 3)
            wait("scalar", "ve", cnt["ve"])
            emit("scalar", lambda e, Y=Y, par=par: e.activation(
                out=fdump, in_=Y, func=AF.Copy, accum_out=sm(f"paS{par}")))
            emit("scalar", lambda e, Y=Y, par=par: e.activation(
                out=fdump, in_=Y, func=AF.Square, accum_out=sm(f"paSS{par}")), "act")
            stats_mark[k] = cnt["act"]

        NSTG = (max(NKD * TCORE, NKI * CAP) // 2) // D - 1

        def f_fb(k):
            par = k % 2
            Y = u32(par * D, D)
            if NSTG >= 3:
                of = bigA.ap().bitcast(F32)[:, (1 + k % 3) * D:(2 + k % 3) * D]
            else:
                of = Y
            wait("vector", "act", stats_mark[k])
            stats_fin_g(f"paS{par}", f"paSS{par}", sm("s2"))
            if NSTG >= 3 and k >= 3:
                wait("vector", *out_mark[k - 3])
            if NSTG < 3 and k >= 2:
                wait("vector", *out_mark[k - 2])
            emit("vector", lambda e, Y=Y, of=of: e.tensor_scalar(
                out=of, in0=Y, scalar1=sm("mS"), scalar2=sm("s2"),
                op0=OP.subtract, op1=OP.mult), "ve")
            fb_apply_ve[k] = cnt["ve"]
            wait("vector", *xrm_mark[k])
            emit("vector", lambda e, of=of, par=par: e.tensor_tensor(
                out=of, in0=of, in1=sx[:, par * D:(par + 1) * D], op=OP.add), "ve")
            x_slot_free[par] = ("ve", cnt["ve"])
            wait("sync", "ve", cnt["ve"])
            out_mark[k] = sdma("outp", "sync",
                               lambda e, k=k, of=of: e.dma_start(
                                   out=y_out[k * 128:(k + 1) * 128, :], in_=of))
            if k + 2 < NTC:
                f_xrm(k + 2)

        def f_xrm(k):
            par = k % 2
            wait("sync", *x_slot_free[par])
            xrm_mark[k] = sdma(f"x{par}", "sync",
                               lambda e, k=k, par=par: e.dma_start(
                                   out=sx[:, par * D:(par + 1) * D],
                                   in_=xr1_sp[k * 128:(k + 1) * 128, :]))

        for db in range(ND4):
            for i in range(NKIS):
                seq, wsl, wmark = load_slice(
                    wsd_in[i][:, db * 512:(db + 1) * 512], 512)
                wait("tensor", *wmark)
                for tc in range(NTC):
                    if i == 0:
                        pe_take_bank(tc % 8)
                    emit("tensor",
                         lambda e, bk=tc % 8, i=i, tc=tc, w=wsl:
                         e.matmul(pf[bk][:, :],
                                  bigB[:, i * TCORE + tc * 128: i * TCORE + tc * 128 + 128],
                                  w[:, 0:512],
                                  start=(i == 0), stop=(i == NKIS - 1)),
                         inc="mm" if tc == NTC - 1 else None)
                slice_consumed(seq, "mm", cnt["mm"])
            mmdb = cnt["mm"]
            if db == ND4 - 1:
                for k in range(min(3, NTC)):
                    f_gather(k)
                f_xrm(0)
                f_xrm(1)
            wait("vector", "mm", mmdb)
            for tc in range(NTC):
                emit("vector", lambda e, bk=tc % 8, tc=tc, db=db: e.tensor_copy(
                    out=bigC[:, tc * D + db * 512: tc * D + db * 512 + 512],
                    in_=pf[bk][:, :]), "ve")
                consumer_frees(tc % 8, "ve", cnt["ve"])
        for k in range(NTC):
            f_fa(k)
            if k >= 1:
                f_fb(k - 1)
        f_fb(NTC - 1)
        if cfg.get("debug"):
            dbg_map = nc.dram_tensor("dbg_map", [MROW, 128], F32, kind="ExternalOutput").ap()
            dbg_ag2 = nc.dram_tensor("dbg_ag2", [NH * E * CAP, DH], BF16, kind="ExternalOutput").ap()
            dbg_xr1 = nc.dram_tensor("dbg_xr1", [TCORE, D], F32, kind="ExternalOutput").ap()
            dbg_ag1 = nc.dram_tensor("dbg_ag1", [TTOK, D], BF16, kind="ExternalOutput").ap()
            wait("sync", "outp", cnt["outp"])
            sdma("exr", "sync", lambda e: e.dma_start(out=dbg_map, in_=map_out_t))
            sdma("exr", "sync", lambda e: e.dma_start(out=dbg_ag2, in_=ag2_out))
            sdma("exr", "sync", lambda e: e.dma_start(out=dbg_xr1, in_=xr1_sp))
            sdma("exr", "sync", lambda e: e.dma_start(out=dbg_ag1, in_=ag1_out))
            wait("sync", "exr", cnt["exr"])
        wait("sync", "outp", cnt["outp"])

        block = ctx.enter_context(nc.Block())

        @block.sync
        def _(eng):
            for f in Q["sync"]:
                f(eng)

        @block.tensor
        def _(eng):
            for f in Q["tensor"]:
                f(eng)

        @block.vector
        def _(eng):
            for f in Q["vector"]:
                f(eng)

        @block.scalar
        def _(eng):
            for f in Q["scalar"]:
                f(eng)

        @block.gpsimd
        def _(eng):
            for f in Q["gpsimd"]:
                f(eng)

    return nc



def _prep_inputs(cfg, inp):
    D = cfg["D"]; TCORE = cfg["TCORE"]; CAP = cfg["CAP"]; E = cfg["E"]; NCORE = cfg["NCORE"]
    NKD = D // 128
    EPC = E // NCORE
    NSC = CAP // 128
    f32 = np.float32
    x = np.ascontiguousarray(np.asarray(inp["L0_attn_x"], dtype=f32).reshape(-1, D))

    def tile_lhsT(w):
        rows, colsn = w.shape
        return np.ascontiguousarray(
            w.reshape(rows // 128, 128, colsn // 128, 128).transpose(2, 1, 0, 3))

    def tile_rhs(w):
        return np.ascontiguousarray(w.reshape(w.shape[0] // 128, 128, w.shape[1]))

    mixW = {m: tile_lhsT(np.ascontiguousarray(np.asarray(inp[f"L0_attn_{m}_w"], f32).T))
            for m in ("A", "B", "D", "Ca", "Cb")}
    wo = tile_rhs(np.ascontiguousarray(np.asarray(inp["L0_attn_out_w"], f32).T))
    NNC = wo.shape[0]
    Ab = np.ascontiguousarray(np.asarray(inp["L0_attn_A_b"], f32).reshape(NNC, 128).T)
    rmsr = np.ascontiguousarray(np.broadcast_to(np.asarray(inp["L0_attn_rms_w"], f32),
                                                (128, D)))
    outbr = np.ascontiguousarray(np.broadcast_to(np.asarray(inp["L0_attn_out_b"], f32),
                                                 (128, D)))
    gate = np.asarray(inp["L0_ffn_gate_w"], f32)
    gatet = np.ascontiguousarray(gate.T.reshape(NKD, 128, E).transpose(1, 0, 2))
    wsg = tile_lhsT(np.asarray(inp["L0_ffn_ws_gate"], f32))
    wsu = tile_lhsT(np.asarray(inp["L0_ffn_ws_up"], f32))
    wsd = tile_rhs(np.asarray(inp["L0_ffn_ws_down"], f32))
    wg = np.asarray(inp["L0_ffn_w_gate"], f32)
    wu = np.asarray(inp["L0_ffn_w_up"], f32)
    wd = np.asarray(inp["L0_ffn_w_down"], f32)
    lt = np.triu(np.ones((128, 128), np.float32), 1).astype(ml_dtypes.bfloat16)
    idn = np.eye(128, dtype=ml_dtypes.bfloat16)
    idnf = np.eye(128, dtype=np.float32)

    in_maps = []
    for c in range(NCORE):
        im = dict(
            x_in=np.ascontiguousarray(x[c * TCORE:(c + 1) * TCORE]),
            wo=wo, Ab=Ab, rmsr=rmsr, outbr=outbr, gatet=gatet,
            wsg=wsg, wsu=wsu, wsd=wsd, lt=lt, idn=idn, idnf=idnf,
            tokb=np.full((128, 1), c * TCORE + 1, f32),
            mapb=np.full((128, 1), c * EPC * NSC, f32),
            pmr=np.ascontiguousarray(np.broadcast_to(
                np.array([1.0 if (cc // (NCORE // 2) == c // (NCORE // 2) and cc < c)
                          else 0.0 for cc in range(NCORE)], f32), (128, NCORE))),
            apmr=np.ascontiguousarray(np.broadcast_to(
                np.array([1.0 if cc < c else 0.0 for cc in range(NCORE)], f32),
                (128, NCORE))),
        )
        for m in ("A", "B", "D", "Ca", "Cb"):
            im[f"w{m}"] = mixW[m]
        for j in range(EPC):
            eidx = c * EPC + j
            im[f"wg{j}"] = tile_lhsT(wg[eidx])
            im[f"wu{j}"] = tile_lhsT(wu[eidx])
            im[f"wd{j}"] = tile_rhs(wd[eidx])
        in_maps.append(im)
    return in_maps


def kernel(**inputs):
    cfg = FULL
    in_maps = _prep_inputs(cfg, inputs)
    nc = build(cfg)
    res = run_bass_kernel_spmd(nc, in_maps, core_ids=list(range(cfg["NCORE"])))
    out = np.concatenate([r["y_out"] for r in res.results], axis=0)
    B, L, Dm = np.asarray(inputs["L0_attn_x"]).shape
    return out.reshape(B, L, Dm).astype(np.float32)


# version 28
# speedup vs baseline: 1.0413x; 1.0413x over previous
import numpy as np
import ml_dtypes
from contextlib import ExitStack
import concourse.bass as bass
import concourse.mybir as mybir
from concourse.bass import IndirectOffsetOnAxis
from concourse.bass_utils import run_bass_kernel_spmd

F32 = mybir.dt.float32
BF16 = mybir.dt.bfloat16
I32 = mybir.dt.int32
U32 = mybir.dt.uint32
AF = mybir.ActivationFunctionType
OP = mybir.AluOpType
AX = mybir.AxisListType.X

FULL = dict(D=2048, NN=2048, TCORE=1024, I_EXP=1408, IS=2816, CAP=1024, E=16, NCORE=8)
TINY = dict(D=512, NN=512, TCORE=256, I_EXP=256, IS=512, CAP=256, E=16, NCORE=8)
EPS = 1e-5


def build(cfg):
    D = cfg["D"]; NN = cfg["NN"]; TCORE = cfg["TCORE"]; I_EXP = cfg["I_EXP"]
    IS = cfg["IS"]; CAP = cfg["CAP"]; E = cfg["E"]; NCORE = cfg["NCORE"]
    NTC = TCORE // 128
    NKD = D // 128
    NNC = NN // 128
    NKI = I_EXP // 128
    NKIS = IS // 128
    NTB = max(1, TCORE // 512)
    TBS = TCORE // NTB
    NSB = max(1, CAP // 512)
    SBS = CAP // NSB
    ND4 = D // 512
    NH = 2 if (ND4 % 2 == 0 and ND4 >= 2) else 1
    DH = D // NH
    EPC = E // NCORE
    NSC = CAP // 128
    TTOK = NCORE * TCORE
    MROW = E * CAP // 128
    WRN = 5

    nc = bass.Bass("TRN2", target_bir_lowering=False)

    def dram_in(name, shape, dt=F32):
        return nc.dram_tensor(name, shape, dt, kind="ExternalInput").ap()

    x_in = dram_in("x_in", [TCORE, D])
    wmix = {m: dram_in(f"w{m}", [NNC, 128, NKD, 128]) for m in ("A", "B", "D", "Ca", "Cb")}
    wo_in = dram_in("wo", [NNC, 128, D])
    Ab_in = dram_in("Ab", [128, NNC])
    rms_in = dram_in("rmsr", [128, D])
    outb_in = dram_in("outbr", [128, D])
    gate_in = dram_in("gatet", [128, NKD, E])
    wsg_in = dram_in("wsg", [NKIS, 128, NKD, 128])
    wsu_in = dram_in("wsu", [NKIS, 128, NKD, 128])
    wsd_in = dram_in("wsd", [NKIS, 128, D])
    wg_in = [dram_in(f"wg{j}", [NKI, 128, NKD, 128]) for j in range(EPC)]
    wu_in = [dram_in(f"wu{j}", [NKI, 128, NKD, 128]) for j in range(EPC)]
    wd_in = [dram_in(f"wd{j}", [NKI, 128, D]) for j in range(EPC)]
    lt_in = dram_in("lt", [128, 128], BF16)
    idn_in = dram_in("idn", [128, 128], BF16)
    idnf_in = dram_in("idnf", [128, 128], F32)
    tokb_in = dram_in("tokb", [128, 1])
    mapb_in = dram_in("mapb", [128, 1])
    pm_in = dram_in("pmr", [128, NCORE])
    apm_in = dram_in("apmr", [128, NCORE])

    y_out = nc.dram_tensor("y_out", [TCORE, D], F32, kind="ExternalOutput").ap()

    ex_in_a = nc.dram_tensor("ex_in_a", [128, NNC], F32).ap()
    ex_in_b = nc.dram_tensor("ex_in_b", [128, NNC], F32).ap()
    ex_out_a = nc.dram_tensor("ex_out_a", [NCORE * 128, NNC], F32, addr_space="Shared").ap()
    ex_out_b = nc.dram_tensor("ex_out_b", [NCORE * 128, NNC], F32, addr_space="Shared").ap()
    cnt_in = nc.dram_tensor("cnt_in", [1, E], F32).ap()
    cnt_out = nc.dram_tensor("cnt_out", [NCORE, E], F32, addr_space="Shared").ap()
    map_in_t = nc.dram_tensor("map_in", [4 * MROW, 128], F32).ap()
    map_out_t = nc.dram_tensor("map_out", [MROW, 128], F32).ap()
    ag1_in = nc.dram_tensor("ag1_in", [TCORE, D], BF16).ap()
    ag1_out = nc.dram_tensor("ag1_out", [TTOK, D], BF16, addr_space="Shared").ap()
    ag2_in = nc.dram_tensor("ag2_in", [NH * EPC * CAP, DH], BF16).ap()
    ag2_out = nc.dram_tensor("ag2_out", [NH * E * CAP, DH], BF16, addr_space="Shared").ap()
    xr1_sp = nc.dram_tensor("xr1_sp", [TCORE, D], F32).ap()

    with ExitStack() as ctx:
        def sbt(name, shape, dt):
            return ctx.enter_context(nc.sbuf_tensor(name, shape, dt))

        bigA = sbt("bigA", [128, max(NKD * TCORE, NKI * CAP)], BF16)
        bigB = sbt("bigB", [128, max(NNC, NKIS) * TCORE], BF16)
        bigC = sbt("bigC", [128, max(NNC * TCORE, NKD * CAP)], BF16)
        sx = sbt("sx", [128, 2 * D], F32)
        un32 = sbt("un32", [128, max(5 * TCORE, 2 * D)], F32)
        zeros = sbt("zeros", [128, max(TCORE, 128)], F32)
        WSL = max(NKD, NKIS, NKI) * 128
        wring = sbt("wring", [128, WRN * WSL], BF16)
        rmsr = sbt("rmsr_s", [128, D], BF16)
        outbr = sbt("outbr_s", [128, D], BF16)
        xb16 = sbt("xb16", [128, 2 * D], BF16)
        slt = sbt("slt", [128, 128], BF16)
        sidn = sbt("sidn", [128, 128], BF16)
        sidnf = sbt("sidnf", [128, 128], F32)
        sones = sbt("sones", [128, 128], BF16)
        sgateb = sbt("sgateb", [128, NKD * E], BF16)
        sAb = sbt("sAb", [128, NNC], F32)
        sm32 = sbt("sm32", [128, 768], F32)
        s_rout = sbt("s_rout", [128, 8 * NTC], F32)
        s_scat = sbt("s_scat", [128, 2 * NTC], I32)
        s_gidx = sbt("s_gidx", [128, 2 * NTC], I32)
        s_gidx2 = sbt("s_gidx2", [128, 2 * NTC], I32)
        s_sval = sbt("s_sval", [128, NTC], F32)
        s_ex = sbt("s_ex", [128, NCORE * 2 * NNC], F32)
        s_map16 = sbt("s_map16", [EPC * NSC, 128], F32)
        s_mapT = sbt("s_mapT", [128, EPC * NSC], F32)
        s_migx = sbt("s_migx", [128, EPC * NSC], I32)
        s_m16 = sbt("s_m16", [EPC * NSC, 1], I32)
        s_m16b = sbt("s_m16b", [EPC * NSC, 1], I32)
        smagic = sbt("smagic", [128, 1], I32)
        siota = sbt("siota", [128, E], F32)
        spid = sbt("spid", [128, 1], F32)
        scnt = sbt("scnt", [1, NCORE * E], F32)
        g_bf = sbt("g_bf", [128, 2 * D], BF16)
        gdump = g_bf.ap().bitcast(F32)[:, 0:D]
        pm_s = sbt("pm_s", [128, NCORE], F32)
        apm_s = sbt("apm_s", [128, NCORE], F32)
        tokb_s = sbt("tokb_s", [128, 1], F32)
        mapb_s = sbt("mapb_s", [128, 1], F32)

        _smoff = [0]; _smmap = {}
        def sm(name, w=1):
            if name not in _smmap:
                _smmap[name] = (_smoff[0], w); _smoff[0] += w
                assert _smoff[0] <= 768, "sm32 overflow"
            o, ww = _smmap[name]
            assert ww == w
            return sm32[:, o:o + w]

        U = un32
        def u32(off, w):
            return U[:, off:off + w]

        pf = [ctx.enter_context(nc.psum_tensor(f"pf{i}", [128, 512], F32)) for i in range(8)]
        pbf = [pf[6].ap().bitcast(BF16), pf[7].ap().bitcast(BF16)]

        semn = (["mm", "ve", "act", "pcm", "cc", "setup", "x0", "x1",
                 "spl", "ag1s", "ag2s", "exr", "outp", "mz", "mz2", "setupg"]
                + [f"wt{i}" for i in range(WRN)])
        sem = {n: ctx.enter_context(nc.semaphore("s_" + n)) for n in semn}
        cnt = {n: 0 for n in semn}
        Q = {k: [] for k in ("tensor", "vector", "scalar", "gpsimd", "sync")}

        def wait(eng, s, v):
            if v > 0:
                Q[eng].append(lambda e, s=s, v=v: e.wait_ge(sem[s], v))

        def bump(s, k=1):
            cnt[s] += k
            return cnt[s]

        def drain(eng):
            Q[eng].append(lambda e: e.drain())

        pool_drain = [False]

        def emit(eng, fn, inc=None, inck=1):
            if inc is None:
                Q[eng].append(lambda e, f=fn: f(e))
            else:
                bump(inc, inck)
                Q[eng].append(lambda e, f=fn, s=inc, k=inck: f(e).then_inc(sem[s], k))
            if eng in ("vector", "scalar") or (eng == "gpsimd" and pool_drain[0]):
                Q[eng].append(lambda e: e.drain())

        def sdma(stream, eng, fn):
            wait(eng, stream, cnt[stream])
            emit(eng, fn, stream, 16)
            return (stream, cnt[stream])

        def cc_issue(fn):
            wait("gpsimd", "cc", cnt["cc"])
            emit("gpsimd", fn, "cc", 1)
            return cnt["cc"]

        bank_free = {b: None for b in range(8)}

        def pe_take_bank(b):
            if bank_free[b] is not None:
                s, v = bank_free[b]
                wait("tensor", s, v)
                bank_free[b] = None

        def consumer_frees(b, s, v):
            bank_free[b] = (s, v)

        ring = {"seq": 0, "cons": {}}

        def load_slice(src_ap, width):
            s = ring["seq"]; ring["seq"] += 1
            slot = s % WRN
            if s >= WRN:
                c = ring["cons"].get(s - WRN)
                assert c is not None, f"ring slot {s} reused before consumer set"
                wait("gpsimd", c[0], c[1])
            dst = wring[:, slot * WSL:slot * WSL + width]
            mark = sdma(f"wt{slot}", "gpsimd",
                        lambda e, d=dst, sa=src_ap: e.dma_start(out=d, in_=sa))
            return s, dst, mark

        def slice_consumed(seq, s, v):
            ring["cons"][seq] = (s, v)

        for dst, src in [(slt, lt_in), (sidn, idn_in),
                         (sidnf, idnf_in), (sAb, Ab_in), (tokb_s, tokb_in),
                         (mapb_s, mapb_in), (pm_s, pm_in), (apm_s, apm_in)]:
            emit("sync", lambda e, d=dst, s=src: e.dma_start(out=d[:, :], in_=s),
                 "setup", 16)
        setup_t = cnt["setup"]
        for dst, src in [(rmsr, rms_in), (outbr, outb_in)]:
            emit("gpsimd", lambda e, d=dst, s=src: e.dma_start(out=d[:, :], in_=s),
                 "setupg", 16)
        emit("gpsimd", lambda e: e.dma_start(
            out=sgateb[:, :], in_=gate_in.rearrange("p k e -> p (k e)")), "setupg", 16)
        setup_g = cnt["setupg"]

        wait("vector", "setup", setup_t)
        wait("vector", "setupg", setup_g)
        emit("vector", lambda e: e.memset(zeros[:, :], 0.0))
        emit("vector", lambda e: e.memset(smagic[:, :], 0x5F3759DF))
        emit("vector", lambda e: e.memset(sones[:, :], 1.0), "ve")
        ve_setup = cnt["ve"]
        wait("gpsimd", "ve", ve_setup)
        pool_drain[0] = True
        emit("gpsimd", lambda e: e.iota(out=siota[:, :], pattern=[[1, E]], base=0,
                                        channel_multiplier=0,
                                        allow_small_or_imprecise_dtypes=True))
        emit("gpsimd", lambda e: e.iota(out=spid[:, :], pattern=[[0, 1]], base=0,
                                        channel_multiplier=1,
                                        allow_small_or_imprecise_dtypes=True))
        emit("gpsimd", lambda e: e.iota(out=s_m16[:, :], pattern=[[0, 1]], base=0,
                                        channel_multiplier=1,
                                        allow_small_or_imprecise_dtypes=True))
        emit("gpsimd", lambda e: e.iota(out=s_sval[:, :], pattern=[[128, NTC]], base=0,
                                        channel_multiplier=1,
                                        allow_small_or_imprecise_dtypes=True), "pcm")
        pool_drain[0] = False
        pcm_setup = cnt["pcm"]
        wait("vector", "pcm", pcm_setup)
        emit("vector", lambda e: e.tensor_scalar(out=spid[:, :], in0=spid[:, :],
                                                 scalar1=tokb_s[:, 0:1], scalar2=None,
                                                 op0=OP.add))
        emit("vector", lambda e: e.tensor_scalar(out=s_sval[:, :], in0=s_sval[:, :],
                                                 scalar1=tokb_s[:, 0:1], scalar2=None,
                                                 op0=OP.add))
        emit("vector", lambda e: e.tensor_copy(out=s_m16b[:, :],
                                               in_=mapb_s[0:EPC * NSC, 0:1]))
        drain("vector")
        emit("vector", lambda e: e.tensor_tensor(out=s_m16[:, :], in0=s_m16[:, :],
                                                 in1=s_m16b[:, :], op=OP.add), "ve")
        drain("vector")
        m16_done = cnt["ve"]

        def newton_rsqrt(dst, src, eng="vector", pfx=""):
            t1, ti = sm(pfx + "nt1"), sm(pfx + "nti")
            emit(eng, lambda e: e.tensor_scalar(out=ti.bitcast(I32),
                                                in0=src.bitcast(I32),
                                                scalar1=1, scalar2=None,
                                                op0=OP.logical_shift_right))
            drain(eng)
            emit(eng, lambda e: e.tensor_scalar(out=ti.bitcast(I32),
                                                in0=ti.bitcast(I32),
                                                scalar1=-1, scalar2=None,
                                                op0=OP.bitwise_xor))
            drain(eng)
            emit(eng, lambda e: e.tensor_scalar(out=dst.bitcast(I32),
                                                in0=ti.bitcast(I32),
                                                scalar1=0x5F3759DF + 1, scalar2=None,
                                                op0=OP.add))
            drain(eng)
            for _ in range(2):
                emit(eng, lambda e: e.tensor_tensor(out=t1, in0=dst, in1=dst,
                                                    op=OP.mult))
                drain(eng)
                emit(eng, lambda e: e.tensor_tensor(out=t1, in0=t1, in1=src,
                                                    op=OP.mult))
                drain(eng)
                emit(eng, lambda e: e.tensor_scalar(out=t1, in0=t1, scalar1=-0.5,
                                                    scalar2=1.5, op0=OP.mult,
                                                    op1=OP.add))
                drain(eng)
                emit(eng, lambda e: e.tensor_tensor(out=dst, in0=dst, in1=t1,
                                                    op=OP.mult))
                drain(eng)

        def ln_stats(src_ap, dump_ap, s_dst, with_rms=False):
            wait("scalar", "ve", cnt["ve"])
            emit("scalar", lambda e: e.activation(out=dump_ap, in_=src_ap, func=AF.Copy,
                                                  accum_out=sm("aS")))
            drain("scalar")
            emit("scalar", lambda e: e.activation(out=dump_ap, in_=src_ap, func=AF.Square,
                                                  accum_out=sm("aSS")), "act")
            wait("vector", "act", cnt["act"])
            emit("vector", lambda e: e.tensor_scalar(out=sm("mS"), in0=sm("aS"),
                                                     scalar1=1.0 / D, scalar2=None,
                                                     op0=OP.mult))
            drain("vector")
            emit("vector", lambda e: e.tensor_scalar(out=sm("eS"), in0=sm("aSS"),
                                                     scalar1=1.0 / D, scalar2=None,
                                                     op0=OP.mult))
            drain("vector")
            emit("vector", lambda e: e.tensor_tensor(out=sm("m2S"), in0=sm("mS"),
                                                     in1=sm("mS"), op=OP.mult))
            drain("vector")
            emit("vector", lambda e: e.tensor_tensor(out=sm("vS"), in0=sm("eS"),
                                                     in1=sm("m2S"), op=OP.subtract))
            drain("vector")
            if with_rms:
                emit("vector", lambda e: e.tensor_scalar(out=sm("qS"), in0=sm("vS"),
                                                         scalar1=1.0 + EPS,
                                                         scalar2=EPS * EPS,
                                                         op0=OP.mult, op1=OP.add))
            else:
                emit("vector", lambda e: e.tensor_scalar(out=sm("qS"), in0=sm("vS"),
                                                         scalar1=1.0, scalar2=EPS,
                                                         op0=OP.mult, op1=OP.add))
            drain("vector")
            newton_rsqrt(s_dst, sm("qS"))

        def stats_fin_g(aname, ssname, dst, rms=False, eng="vector", pfx="",
                        mdst=None):
            mS = sm(pfx + "mS") if mdst is None else mdst
            emit(eng, lambda e: e.tensor_scalar(
                out=mS, in0=sm(aname), scalar1=1.0 / D, scalar2=None,
                op0=OP.mult))
            emit(eng, lambda e: e.tensor_scalar(
                out=sm(pfx + "eS"), in0=sm(ssname), scalar1=1.0 / D, scalar2=None,
                op0=OP.mult))
            emit(eng, lambda e: e.tensor_tensor(out=sm(pfx + "m2S"), in0=mS,
                                                in1=mS, op=OP.mult))
            emit(eng, lambda e: e.tensor_tensor(out=sm(pfx + "vS"),
                                                in0=sm(pfx + "eS"),
                                                in1=sm(pfx + "m2S"), op=OP.subtract))
            if rms:
                emit(eng, lambda e: e.tensor_scalar(out=sm(pfx + "qS"),
                                                    in0=sm(pfx + "vS"),
                                                    scalar1=1.0 + EPS,
                                                    scalar2=EPS * EPS,
                                                    op0=OP.mult, op1=OP.add))
            else:
                emit(eng, lambda e: e.tensor_scalar(out=sm(pfx + "qS"),
                                                    in0=sm(pfx + "vS"),
                                                    scalar1=1.0, scalar2=EPS,
                                                    op0=OP.mult, op1=OP.add))
            newton_rsqrt(dst, sm(pfx + "qS"), eng=eng, pfx=pfx)

        tp = {"fill": 0}

        def transpose_tile(src_tile, dst_fn, nblk, pre_wait=None):
            mm_mark = 0
            for half in range((nblk + 7) // 8):
                f = tp["fill"]; tp["fill"] += 1
                pb = f % 2
                pe_take_bank(6 + pb)
                if pre_wait is not None:
                    wait("tensor", *pre_wait)
                    pre_wait = None
                nb = min(8, nblk - half * 8)
                for j in range(nb):
                    dc = half * 8 + j
                    emit("tensor",
                         lambda e, pb=pb, j=j, dc=dc, st=src_tile:
                         e.transpose(out=pbf[pb][:, j * 128:(j + 1) * 128],
                                     in_=st[:, dc * 128:(dc + 1) * 128],
                                     identity=sidn[:, :]),
                         inc="mm" if j == nb - 1 else None)
                mm_mark = cnt["mm"]
                wait("vector", "mm", mm_mark)
                for j in range(nb):
                    dc = half * 8 + j
                    emit("vector",
                         lambda e, pb=pb, j=j, d=dst_fn(dc):
                         e.tensor_copy(out=d, in_=pbf[pb][:, j * 128:(j + 1) * 128]),
                         inc="ve" if j == nb - 1 else None)
                consumer_frees(6 + pb, "ve", cnt["ve"])
            return cnt["ve"], mm_mark

        x_slot_free = {0: ("ve", 0), 1: ("ve", 0)}
        xb_slot_free = {0: 0, 1: 0}
        m0_sa = {}
        m0_xm = {}
        tpve_k = {}

        def m0_x(k):
            par = k % 2
            wait("sync", *x_slot_free[par])
            m0_xm[k] = sdma(f"x{par}", "sync",
                            lambda e, k=k, par=par: e.dma_start(
                                out=sx[:, par * D:(par + 1) * D],
                                in_=x_in[k * 128:(k + 1) * 128, :]))

        def m0_a(k):
            par = k % 2
            xt = sx[:, par * D:(par + 1) * D]
            wait("scalar", *m0_xm[k])
            emit("scalar", lambda e, xt=xt, par=par: e.activation(
                out=gdump, in_=xt, func=AF.Copy, accum_out=sm(f"paS{par}")))
            emit("scalar", lambda e, xt=xt, par=par: e.activation(
                out=gdump, in_=xt, func=AF.Square, accum_out=sm(f"paSS{par}")), "act")
            m0_sa[k] = cnt["act"]

        def m0_b(k):
            par = k % 2
            xt = sx[:, par * D:(par + 1) * D]
            wait("vector", "act", m0_sa[k])
            stats_fin_g(f"paS{par}", f"paSS{par}", sm("q1"), rms=True)
            emit("vector", lambda e, k=k: e.tensor_copy(
                out=sm("m1s", NTC)[:, k:k + 1], in_=sm("mS")))
            emit("vector", lambda e, k=k: e.tensor_copy(
                out=sm("s1s", NTC)[:, k:k + 1], in_=sm("q1")))
            emit("vector", lambda e, xt=xt: e.tensor_scalar(
                out=u32(0, D), in0=xt, scalar1=sm("mS"),
                scalar2=sm("q1"), op0=OP.subtract, op1=OP.mult), "ve")
            x_slot_free[par] = ("ve", cnt["ve"])
            if k + 2 < NTC:
                m0_x(k + 2)
            wait("vector", "mm", xb_slot_free[par])
            emit("vector", lambda e, par=par: e.tensor_tensor(
                out=xb16[:, par * D:(par + 1) * D], in0=u32(0, D), in1=rmsr[:, :],
                op=OP.mult), "ve")
            xb_ready = cnt["ve"]
            vem, mmm = transpose_tile(
                xb16[:, par * D:],
                lambda dc, k=k: bigA[:, dc * TCORE + k * 128: dc * TCORE + k * 128 + 128],
                NKD, pre_wait=("ve", xb_ready))
            xb_slot_free[par] = mmm
            tpve_k[k] = vem

        def mixw_rhs(dc, tb):
            return bigA[:, dc * TCORE + tb * TBS: dc * TCORE + (tb + 1) * TBS]

        def proj_tb(pair, ws, tb, pre_ve):
            seq, wsl, wmark = ws
            wait("tensor", *wmark)
            wait("tensor", "ve", pre_ve)
            pe_take_bank(pair[tb % 2])
            for dc in range(NKD):
                emit("tensor",
                     lambda e, bk=pair[tb % 2], dc=dc, tb=tb, w=wsl:
                     e.matmul(pf[bk][:, :TBS], w[:, dc * 128:(dc + 1) * 128],
                              mixw_rhs(dc, tb), start=(dc == 0),
                              stop=(dc == NKD - 1)),
                     inc="mm" if dc == NKD - 1 else None)
            return cnt["mm"]

        m1_pre = NTB == 2 and NTC >= 8
        if m1_pre:
            pre_m1 = {m: load_slice(wmix[m][0].rearrange("p k n -> p (k n)"),
                                    NKD * 128) for m in ("A", "B", "D")}
        pre_mm = {}
        m0_x(0)
        m0_x(1)
        for k in range(NTC):
            if k >= 1:
                m0_b(k - 1)
            if m1_pre and k in (5, 6, 7):
                m, pr = (("A", (0, 1)), ("B", (2, 3)), ("D", (4, 5)))[k - 5]
                pre_mm[m] = proj_tb(pr, pre_m1[m], 0, tpve_k[3])
            m0_a(k)
        m0_b(NTC - 1)
        if m1_pre:
            for m, pr in (("A", (0, 1)), ("B", (2, 3)), ("D", (4, 5))):
                pre_mm[m] = proj_tb(pr, pre_m1[m], 1, tpve_k[NTC - 1])
                slice_consumed(pre_m1[m][0], "mm", pre_mm[m])
        m0_ve = cnt["ve"]

        A_ = u32(0, TCORE); Uu_ = u32(TCORE, TCORE); DS_ = u32(2 * TCORE, TCORE)
        CS_ = DS_
        H_ = u32(3 * TCORE, TCORE); P_ = u32(4 * TCORE, TCORE)

        def hsl(buf, tb):
            return buf[:, tb * TBS:(tb + 1) * TBS]

        def proj_job(jid, ws, rhs_fn, blklen, nblkm, kchunks):
            seq, wsl, wmark = ws
            pair = (2 * (jid % 3), 2 * (jid % 3) + 1)
            wait("tensor", *wmark)
            for i in range(min(2, nblkm)):
                pe_take_bank(pair[i])
            for dc in range(kchunks):
                for tb in range(nblkm):
                    bk = pair[tb % 2]
                    emit("tensor",
                         lambda e, bk=bk, dc=dc, tb=tb, w=wsl, bl=blklen:
                         e.matmul(pf[bk][:, :bl], w[:, dc * 128:(dc + 1) * 128],
                                  rhs_fn(dc, tb), start=(dc == 0),
                                  stop=(dc == kchunks - 1)),
                         inc="mm" if (tb == nblkm - 1 and dc == kchunks - 1) else None)
            mmv = cnt["mm"]
            slice_consumed(seq, "mm", mmv)
            return pair, mmv

        wait("tensor", "ve", m0_ve)
        jid = 0
        prev_nc_done = m0_ve
        for nnc in range(NNC):
            if nnc == 0 and m1_pre:
                pa, va = (0, 1), pre_mm["A"]
                pb_, vb = (2, 3), pre_mm["B"]
                pd, vd = (4, 5), pre_mm["D"]
                jid = 3
            else:
                wsA = load_slice(wmix["A"][nnc].rearrange("p k n -> p (k n)"),
                                 NKD * 128)
                pa, va = proj_job(jid, wsA, mixw_rhs, TBS, NTB, NKD); jid += 1
                wsB = load_slice(wmix["B"][nnc].rearrange("p k n -> p (k n)"),
                                 NKD * 128)
                pb_, vb = proj_job(jid, wsB, mixw_rhs, TBS, NTB, NKD); jid += 1
                wsD = load_slice(wmix["D"][nnc].rearrange("p k n -> p (k n)"),
                                 NKD * 128)
                pd, vd = proj_job(jid, wsD, mixw_rhs, TBS, NTB, NKD); jid += 1
            wait("scalar", "ve", prev_nc_done)
            wait("scalar", "mm", va)
            for tb in range(NTB):
                emit("scalar", lambda e, b=pa[tb % 2], tb=tb, n=nnc:
                     e.activation(out=hsl(A_, tb), in_=pf[b][:, :TBS], func=AF.Sigmoid,
                                  bias=sAb[:, n:n + 1], scale=1.0),
                     inc="act" if tb == NTB - 1 else None)
            a_done = cnt["act"]
            for b in set(pa[i % 2] for i in range(NTB)):
                consumer_frees(b, "act", a_done)
            wait("scalar", "mm", vd)
            for tb in range(NTB):
                emit("scalar", lambda e, b=pd[tb % 2], tb=tb:
                     e.activation(out=hsl(DS_, tb), in_=pf[b][:, :TBS], func=AF.Sigmoid),
                     inc="act" if tb == NTB - 1 else None)
            ds_done = cnt["act"]
            for b in set(pd[i % 2] for i in range(NTB)):
                consumer_frees(b, "act", ds_done)
            wait("vector", "mm", vb)
            wait("vector", "act", ds_done)
            for tb in range(NTB):
                emit("vector", lambda e, b=pb_[tb % 2], tb=tb:
                     e.tensor_tensor(out=hsl(Uu_, tb), in0=hsl(DS_, tb),
                                     in1=pf[b][:, :TBS], op=OP.mult),
                     inc="ve" if tb == NTB - 1 else None)
            u_done = cnt["ve"]
            for b in set(pb_[i % 2] for i in range(NTB)):
                consumer_frees(b, "ve", u_done)
            wsCa = load_slice(wmix["Ca"][nnc].rearrange("p k n -> p (k n)"), NKD * 128)
            pc, vc = proj_job(jid, wsCa, mixw_rhs, TBS, NTB, NKD); jid += 1
            wsCb = load_slice(wmix["Cb"][nnc].rearrange("p k n -> p (k n)"), NKD * 128)
            pcb, vcb = proj_job(jid, wsCb, mixw_rhs, TBS, NTB, NKD); jid += 1
            wait("vector", "act", a_done)
            emit("vector", lambda e: e.tensor_tensor_scan(
                out=H_, data0=A_, data1=Uu_, initial=0.0, op0=OP.mult, op1=OP.add))
            emit("vector", lambda e: e.tensor_tensor_scan(
                out=P_, data0=A_, data1=zeros[:, 0:TCORE], initial=1.0,
                op0=OP.mult, op1=OP.add))
            emit("vector", lambda e, n=nnc: e.tensor_copy(
                out=sm("hl", NNC)[:, n:n + 1], in_=H_[:, TCORE - 1:TCORE]))
            emit("vector", lambda e, n=nnc: e.tensor_copy(
                out=sm("pt", NNC)[:, n:n + 1], in_=P_[:, TCORE - 1:TCORE]))
            wait("scalar", "ve", u_done)
            wait("scalar", "mm", vc)
            for tb in range(NTB):
                emit("scalar", lambda e, b=pc[tb % 2], tb=tb:
                     e.activation(out=hsl(CS_, tb), in_=pf[b][:, :TBS], func=AF.Silu),
                     inc="act" if tb == NTB - 1 else None)
            cs_done = cnt["act"]
            for b in set(pc[i % 2] for i in range(NTB)):
                consumer_frees(b, "act", cs_done)
            wait("vector", "mm", vcb)
            wait("vector", "act", cs_done)
            for tb in range(NTB):
                emit("vector", lambda e, b=pcb[tb % 2], tb=tb:
                     e.tensor_tensor(out=hsl(CS_, tb), in0=hsl(CS_, tb),
                                     in1=pf[b][:, :TBS], op=OP.mult),
                     inc="ve" if tb == NTB - 1 else None)
            c_done = cnt["ve"]
            for b in set(pcb[i % 2] for i in range(NTB)):
                consumer_frees(b, "ve", c_done)
            emit("vector", lambda e, n=nnc: e.tensor_tensor(
                out=bigB[:, n * TCORE:(n + 1) * TCORE], in0=CS_, in1=H_, op=OP.mult))
            emit("vector", lambda e, n=nnc: e.tensor_tensor(
                out=bigC[:, n * TCORE:(n + 1) * TCORE], in0=CS_, in1=P_, op=OP.mult), "ve")
            prev_nc_done = cnt["ve"]
            if nnc == NNC // 2 - 1:
                wait("sync", "ve", cnt["ve"])
                sdma("exr", "sync", lambda e: e.dma_start(
                    out=ex_in_a[:, 0:NNC // 2], in_=sm("pt", NNC)[:, 0:NNC // 2]))
                exwa = sdma("exr", "sync", lambda e: e.dma_start(
                    out=ex_in_a[:, NNC // 2:NNC], in_=sm("hl", NNC)[:, 0:NNC // 2]))
                wait("gpsimd", *exwa)
                ex_cc_a = cc_issue(lambda e: e.collective_compute(
                    "AllGather", OP.bypass, replica_groups=[list(range(NCORE))],
                    ins=[ex_in_a.opt()], outs=[ex_out_a.opt()]))
        m1_ve = cnt["ve"]

        NNC2 = NNC // 2
        wait("sync", "ve", m1_ve)
        sdma("exr", "sync", lambda e: e.dma_start(
            out=ex_in_b[:, 0:NNC2], in_=sm("pt", NNC)[:, NNC2:NNC]))
        exwb = sdma("exr", "sync", lambda e: e.dma_start(
            out=ex_in_b[:, NNC2:NNC], in_=sm("hl", NNC)[:, NNC2:NNC]))
        wait("gpsimd", *exwb)
        ex_cc_b = cc_issue(lambda e: e.collective_compute(
            "AllGather", OP.bypass, replica_groups=[list(range(NCORE))],
            ins=[ex_in_b.opt()], outs=[ex_out_b.opt()]))
        y_marks = []

        def m2_fold(h, cc_mark, ex_out_h):
            base = h * NCORE * NNC
            off = h * NNC2
            wait("sync", "cc", cc_mark)
            exr = sdma("exr", "sync", lambda e, base=base, ex_out_h=ex_out_h:
                       e.dma_start(
                           out=s_ex[:, base:base + NCORE * NNC].rearrange(
                               "p (c j) -> p c j", c=NCORE),
                           in_=ex_out_h.rearrange("(c p) j -> p c j", p=128)))
            wait("vector", *exr)
            if h == 0:
                emit("vector", lambda e: e.memset(sm("Hin", NNC), 0.0))
                drain("vector")
                emit("vector", lambda e: e.tensor_scalar(
                    out=sm("pm1", NCORE), in0=pm_s[:, :], scalar1=-1.0, scalar2=1.0,
                    op0=OP.mult, op1=OP.add))
                drain("vector")
            for c in range(NCORE):
                emit("vector", lambda e, c=c, base=base: e.tensor_scalar(
                    out=sm("eP", NNC)[:, 0:NNC2],
                    in0=s_ex[:, base + c * NNC: base + c * NNC + NNC2],
                    scalar1=pm_s[:, c:c + 1], scalar2=sm("pm1", NCORE)[:, c:c + 1],
                    op0=OP.mult, op1=OP.add))
                drain("vector")
                emit("vector", lambda e, c=c, base=base: e.tensor_scalar(
                    out=sm("eh", NNC)[:, 0:NNC2],
                    in0=s_ex[:, base + c * NNC + NNC2: base + (c + 1) * NNC],
                    scalar1=pm_s[:, c:c + 1], scalar2=None, op0=OP.mult))
                drain("vector")
                emit("vector", lambda e, off=off: e.tensor_tensor(
                    out=sm("Hin", NNC)[:, off:off + NNC2],
                    in0=sm("Hin", NNC)[:, off:off + NNC2],
                    in1=sm("eP", NNC)[:, 0:NNC2], op=OP.mult))
                drain("vector")
                emit("vector", lambda e, off=off: e.tensor_tensor(
                    out=sm("Hin", NNC)[:, off:off + NNC2],
                    in0=sm("Hin", NNC)[:, off:off + NNC2],
                    in1=sm("eh", NNC)[:, 0:NNC2], op=OP.add))
                drain("vector")
            for nnc in range(off, off + NNC2):
                emit("vector", lambda e, n=nnc: e.scalar_tensor_tensor(
                    out=bigB[:, n * TCORE:(n + 1) * TCORE],
                    in0=bigC[:, n * TCORE:(n + 1) * TCORE],
                    scalar=sm("Hin", NNC)[:, n:n + 1],
                    in1=bigB[:, n * TCORE:(n + 1) * TCORE],
                    op0=OP.mult, op1=OP.add), "ve")
                y_marks.append(cnt["ve"])

        m2_fold(0, ex_cc_a, ex_out_a)
        m2_fold(1, ex_cc_b, ex_out_b)
        y_ready = cnt["ve"]

        def O32(t01):
            return u32(t01 * D, D)
        DUMP = gdump
        base_init = [False]
        spill_mark = {}
        ag1_mark = {}
        m4_tpve = [0]
        wait("tensor", "setupg", setup_g)
        wait("vector", "act", cnt["act"])
        xmarks = {}

        def m4_x(k):
            par = k % 2
            wait("sync", *x_slot_free[par])
            xmarks[k] = sdma(f"x{par}", "sync",
                             lambda e, k=k, par=par: e.dma_start(
                                 out=sx[:, par * D:(par + 1) * D],
                                 in_=x_in[k * 128:(k + 1) * 128, :]))

        m4_x(0)
        m4_x(1)
        for k in (0, 1):
            wait("vector", *xmarks[k])
            emit("vector", lambda e, k=k: e.tensor_scalar(
                out=u32((k % 2) * D, D), in0=sx[:, (k % 2) * D:(k % 2 + 1) * D],
                scalar1=sm("m1s", NTC)[:, k:k + 1],
                scalar2=sm("s1s", NTC)[:, k:k + 1],
                op0=OP.subtract, op1=OP.mult), "ve")
        for db in range(ND4):
            for i in range(NNC):
                seq, wsl, wmark = load_slice(wo_in[i][:, db * 512:(db + 1) * 512], 512)
                wait("tensor", *wmark)
                if db == 0:
                    wait("tensor", "ve", y_marks[i])
                for tc in range(NTC):
                    if i == 0:
                        pe_take_bank(tc % 8)
                    emit("tensor",
                         lambda e, bk=tc % 8, i=i, tc=tc, w=wsl:
                         e.matmul(pf[bk][:, :],
                                  bigB[:, i * TCORE + tc * 128: i * TCORE + tc * 128 + 128],
                                  w[:, 0:512],
                                  start=(i == 0), stop=(i == NNC - 1)),
                         inc="mm" if tc == NTC - 1 else None)
                slice_consumed(seq, "mm", cnt["mm"])
            mmdb = cnt["mm"]
            wait("vector", "mm", mmdb)
            for tc in range(NTC):
                emit("vector", lambda e, bk=tc % 8, tc=tc, db=db: e.tensor_tensor(
                    out=bigC[:, tc * D + db * 512: tc * D + db * 512 + 512],
                    in0=pf[bk][:, :], in1=outbr[:, db * 512:(db + 1) * 512],
                    op=OP.add), "ve")
                consumer_frees(tc % 8, "ve", cnt["ve"])
        oproj_ve = cnt["ve"]
        sa_mark = {}
        sb_mark = {}

        def stats_fin(par, dst):
            emit("vector", lambda e, par=par: e.tensor_scalar(
                out=sm("mS"), in0=sm(f"paS{par}"), scalar1=1.0 / D, scalar2=None,
                op0=OP.mult))
            emit("vector", lambda e, par=par: e.tensor_scalar(
                out=sm("eS"), in0=sm(f"paSS{par}"), scalar1=1.0 / D, scalar2=None,
                op0=OP.mult))
            emit("vector", lambda e: e.tensor_tensor(out=sm("m2S"), in0=sm("mS"),
                                                     in1=sm("mS"), op=OP.mult))
            emit("vector", lambda e: e.tensor_tensor(out=sm("vS"), in0=sm("eS"),
                                                     in1=sm("m2S"), op=OP.subtract))
            emit("vector", lambda e: e.tensor_scalar(out=sm("qS"), in0=sm("vS"),
                                                     scalar1=1.0, scalar2=EPS,
                                                     op0=OP.mult, op1=OP.add))
            newton_rsqrt(dst, sm("qS"))

        def m4_a(k):
            par = k % 2
            o = u32(par * D, D)
            if k < 2:
                emit("vector", lambda e, o=o, k=k: e.tensor_tensor(
                    out=o, in0=bigC[:, k * D:(k + 1) * D], in1=o, op=OP.add), "ve")
            else:
                wait("vector", *xmarks[k])
                wait("vector", *spill_mark[k - 2])
                emit("vector", lambda e, k=k, par=par: e.tensor_scalar(
                    out=DUMP, in0=sx[:, par * D:(par + 1) * D],
                    scalar1=sm("m1s", NTC)[:, k:k + 1],
                    scalar2=sm("s1s", NTC)[:, k:k + 1],
                    op0=OP.subtract, op1=OP.mult))
                emit("vector", lambda e, o=o, k=k: e.tensor_tensor(
                    out=o, in0=bigC[:, k * D:(k + 1) * D], in1=DUMP, op=OP.add), "ve")
            wait("scalar", "ve", cnt["ve"])
            emit("scalar", lambda e, o=o, par=par: e.activation(
                out=DUMP, in_=o, func=AF.Copy, accum_out=sm(f"paS{par}")))
            emit("scalar", lambda e, o=o, par=par: e.activation(
                out=DUMP, in_=o, func=AF.Square, accum_out=sm(f"paSS{par}")), "act")
            sa_mark[k] = cnt["act"]

        def m4_b(k):
            par = k % 2
            o = u32(par * D, D)
            wait("vector", "act", sa_mark[k])
            stats_fin(par, sm("s2"))
            emit("vector", lambda e, o=o: e.tensor_scalar(
                out=o, in0=o, scalar1=sm("mS"), scalar2=sm("s2"),
                op0=OP.subtract, op1=OP.mult))
            emit("vector", lambda e, o=o, par=par: e.tensor_tensor(
                out=o, in0=o, in1=sx[:, par * D:(par + 1) * D], op=OP.add), "ve")
            xr_done = cnt["ve"]
            wait("sync", "ve", xr_done)
            spill_mark[k] = sdma("spl", "sync",
                                 lambda e, o=o, k=k: e.dma_start(
                                     out=xr1_sp[k * 128:(k + 1) * 128, :], in_=o))
            wait("scalar", "ve", xr_done)
            emit("scalar", lambda e, o=o, par=par: e.activation(
                out=sx[:, par * D:(par + 1) * D], in_=o, func=AF.Copy,
                accum_out=sm(f"pbS{par}")))
            emit("scalar", lambda e, o=o, par=par: e.activation(
                out=sx[:, par * D:(par + 1) * D], in_=o, func=AF.Square,
                accum_out=sm(f"pbSS{par}")), "act")
            sb_mark[k] = cnt["act"]
            x_slot_free[par] = ("act", sb_mark[k])
            if k + 2 < NTC:
                m4_x(k + 2)

        def stats_fin_b(par, dst):
            emit("vector", lambda e, par=par: e.tensor_scalar(
                out=sm("mS"), in0=sm(f"pbS{par}"), scalar1=1.0 / D, scalar2=None,
                op0=OP.mult))
            emit("vector", lambda e, par=par: e.tensor_scalar(
                out=sm("eS"), in0=sm(f"pbSS{par}"), scalar1=1.0 / D, scalar2=None,
                op0=OP.mult))
            emit("vector", lambda e: e.tensor_tensor(out=sm("m2S"), in0=sm("mS"),
                                                     in1=sm("mS"), op=OP.mult))
            emit("vector", lambda e: e.tensor_tensor(out=sm("vS"), in0=sm("eS"),
                                                     in1=sm("m2S"), op=OP.subtract))
            emit("vector", lambda e: e.tensor_scalar(out=sm("qS"), in0=sm("vS"),
                                                     scalar1=1.0, scalar2=EPS,
                                                     op0=OP.mult, op1=OP.add))
            newton_rsqrt(dst, sm("qS"))

        def m4_c(k):
            par = k % 2
            o = u32(par * D, D)
            wait("vector", "act", sb_mark[k])
            stats_fin_b(par, sm("s2"))
            if k >= 2:
                wait("vector", *ag1_mark[k - 2])
            wait("vector", "mm", xb_slot_free[par])
            emit("vector", lambda e, o=o, par=par: e.tensor_scalar(
                out=xb16[:, par * D:(par + 1) * D], in0=o, scalar1=sm("mS"),
                scalar2=sm("s2"), op0=OP.subtract, op1=OP.mult), "ve")
            xffn_done = cnt["ve"]
            wait("sync", "ve", xffn_done)
            ag1_mark[k] = sdma("ag1s", "sync",
                               lambda e, k=k, par=par: e.dma_start(
                                   out=ag1_in[k * 128:(k + 1) * 128, :],
                                   in_=xb16[:, par * D:(par + 1) * D]))
            if k == NTC // 2 - 1:
                wait("gpsimd", *ag1_mark[k])
                cc_issue(lambda e: e.collective_compute(
                    "AllGather", OP.bypass, replica_groups=[list(range(NCORE))],
                    ins=[ag1_in[0:TCORE // 2, :].opt()],
                    outs=[ag1_out[0:NCORE * (TCORE // 2), :].opt()]))
            tve, tmm = transpose_tile(
                xb16[:, par * D:],
                lambda dc, k=k: bigA[:, dc * TCORE + k * 128:
                                     dc * TCORE + k * 128 + 128],
                NKD, pre_wait=("ve", xffn_done))
            xb_slot_free[par] = tmm
            m4_tpve[0] = tve
            pe_take_bank(5)
            wait("tensor", "ve", tve)
            for dc in range(NKD):
                emit("tensor", lambda e, dc=dc, k=k: e.matmul(
                    pf[5][:, 0:E],
                    bigA[:, dc * TCORE + k * 128: dc * TCORE + k * 128 + 128],
                    sgateb[:, dc * E:(dc + 1) * E],
                    start=(dc == 0), stop=(dc == NKD - 1)),
                    inc="mm" if dc == NKD - 1 else None)
            wait("vector", "mm", cnt["mm"])
            emit("vector", lambda e, k=k: e.tensor_copy(
                out=sm("lgs", NTC * E)[:, k * E:(k + 1) * E], in_=pf[5][:, 0:E]), "ve")
            consumer_frees(5, "ve", cnt["ve"])

        for k in range(NTC):
            if k >= 2:
                m4_c(k - 2)
            if k >= 1:
                m4_b(k - 1)
            m4_a(k)
        m4_b(NTC - 1)
        m4_c(NTC - 2)
        m4_c(NTC - 1)
        rout_ve_h = [0]
        oh_mark = {}

        def routing_part1(tc):
            r = 8 * tc
            lgt = sm("lgs", NTC * E)[:, tc * E:(tc + 1) * E]
            emit("vector", lambda e, lgt=lgt: e.reduce_max(out=sm("rmx"), in_=lgt,
                                                           axis=AX))
            emit("vector", lambda e, lgt=lgt: e.tensor_scalar(
                out=sm("zc", E), in0=lgt, scalar1=sm("rmx"), scalar2=None,
                op0=OP.subtract), "ve")
            wait("scalar", "ve", cnt["ve"])
            emit("scalar", lambda e: e.activation(out=sm("sg", E), in_=sm("zc", E),
                                                  func=AF.Sigmoid), "act")
            wait("vector", "act", cnt["act"])
            emit("vector", lambda e: e.tensor_scalar(out=sm("om", E), in0=sm("sg", E),
                                                     scalar1=-1.0, scalar2=1.0,
                                                     op0=OP.mult, op1=OP.add))
            emit("vector", lambda e: e.reciprocal(out=sm("om", E), in_=sm("om", E)))
            emit("vector", lambda e: e.tensor_tensor(out=sm("esc", E), in0=sm("sg", E),
                                                     in1=sm("om", E), op=OP.mult))
            emit("vector", lambda e: e.reduce_sum(out=sm("ssum"), in_=sm("esc", E),
                                                  axis=AX))
            emit("vector", lambda e: e.reciprocal(out=sm("ssum"), in_=sm("ssum")))
            emit("vector", lambda e: e.tensor_scalar(out=sm("esc", E),
                                                     in0=sm("esc", E),
                                                     scalar1=sm("ssum"), scalar2=None,
                                                     op0=OP.mult))
            emit("vector", lambda e: e.max(out=sm("mx8", 8), in_=sm("esc", E)))
            emit("vector", lambda e: e.max_index(
                out=sm("mi8", 8).bitcast(U32), in_max=sm("mx8", 8),
                in_values=sm("esc", E)))
            emit("vector", lambda e, r=r: e.tensor_copy(out=s_rout[:, r + 2:r + 4],
                                                        in_=sm("mx8", 8)[:, 0:2]))
            emit("vector", lambda e, r=r: e.tensor_copy(
                out=s_rout[:, r:r + 2], in_=sm("mi8", 8).bitcast(U32)[:, 0:2]))
            emit("vector", lambda e, r=r: e.tensor_scalar(
                out=sm("m1e", E), in0=siota[:, :], scalar1=s_rout[:, r:r + 1],
                scalar2=None, op0=OP.is_equal))
            emit("vector", lambda e, r=r: e.tensor_scalar(
                out=sm("m2e", E), in0=siota[:, :], scalar1=s_rout[:, r + 1:r + 2],
                scalar2=None, op0=OP.is_equal))
            emit("vector", lambda e: e.tensor_tensor(
                out=sm("ohb", 8).bitcast(BF16)[:, 0:E], in0=sm("m1e", E),
                in1=sm("m2e", E), op=OP.add), "ve")
            oh_mark[tc] = cnt["ve"]

        def routing_part2(tc):
            r = 8 * tc
            wait("tensor", "ve", oh_mark[tc])
            pe_take_bank(6)
            emit("tensor", lambda e: e.matmul(pf[6][:, 0:E], slt[:, :],
                                              sm("ohb", 8).bitcast(BF16)[:, 0:E],
                                              start=True, stop=True), "mm")
            pe_take_bank(7)
            emit("tensor", lambda e: e.matmul(pf[7][:, 0:E], sones[:, :],
                                              sm("ohb", 8).bitcast(BF16)[:, 0:E],
                                              start=True, stop=True), "mm")
            pos_mm = cnt["mm"]
            wait("vector", "mm", pos_mm)
            if not base_init[0]:
                emit("vector", lambda e: e.memset(sm("brep", E), 0.0))
                base_init[0] = True
            emit("vector", lambda e: e.tensor_tensor(out=sm("pos", E),
                                                     in0=pf[6][:, 0:E],
                                                     in1=sm("brep", E), op=OP.add),
                 "ve")
            consumer_frees(6, "ve", cnt["ve"])
            emit("vector", lambda e: e.tensor_tensor(out=sm("brep", E),
                                                     in0=sm("brep", E),
                                                     in1=pf[7][:, 0:E], op=OP.add),
                 "ve")
            consumer_frees(7, "ve", cnt["ve"])
            emit("vector", lambda e, r=r: e.scalar_tensor_tensor(
                out=sm("tmpe", E), in0=sm("pos", E), scalar=1.0, in1=sm("m1e", E),
                op0=OP.mult, op1=OP.mult, accum_out=s_rout[:, r + 4:r + 5]))
            emit("vector", lambda e, r=r: e.scalar_tensor_tensor(
                out=sm("tmpe", E), in0=sm("pos", E), scalar=1.0, in1=sm("m2e", E),
                op0=OP.mult, op1=OP.mult, accum_out=s_rout[:, r + 5:r + 6]), "ve")
            if tc == NTC - 1:
                rout_ve_h[0] = cnt["ve"]

        NP = 2 * NTC

        def pairs_ap(base, extra_e=False):
            a = s_rout[:, base:]
            dims = [list(a.ap[0]), [8, NTC], [1, 2]] + ([[0, E]] if extra_e else [])
            return bass.AP(a.tensor, a.offset, dims, None)

        m4_act = cnt["act"]

        def m5_counts():
            wait("sync", "ve", rout_ve_h[0])
            cntw = sdma("exr", "sync",
                        lambda e: e.dma_start(out=cnt_in, in_=sm("brep", E)[0:1, :]))
            wait("gpsimd", *cntw)
            cc_issue(lambda e: e.collective_compute(
                "AllGather", OP.bypass, replica_groups=[list(range(NCORE))],
                ins=[cnt_in.opt()], outs=[cnt_out.opt()]))
            wait("sync", "cc", cnt["cc"])
            cntr = sdma("exr", "sync", lambda e: e.dma_start(
                out=scnt[:, :], in_=cnt_out.rearrange("c e -> (c e)").unsqueeze(0)))
            wait("vector", *cntr)
            emit("vector", lambda e: e.memset(sm("cb", E)[0:1, :], 0.0))
            for c in range(NCORE):
                emit("vector", lambda e, c=c: e.scalar_tensor_tensor(
                    out=sm("cb", E)[0:1, :], in0=scnt[0:1, c * E:(c + 1) * E],
                    scalar=apm_s[0:1, c:c + 1], in1=sm("cb", E)[0:1, :],
                    op0=OP.mult, op1=OP.add))
            emit("vector", lambda e: e.memset(sm("o128", 128)[0:1, :], 1.0), "ve")
            return ("ve", cnt["ve"])

        def m5_cbr(cb_mark):
            wait("tensor", *cb_mark)
            pe_take_bank(4)
            emit("tensor", lambda e: e.matmul(pf[4][:, 0:E], sm("o128", 128)[0:1, :],
                                              sm("cb", E)[0:1, :], start=True, stop=True),
                 "mm")
            cbr_mm = cnt["mm"]
            wait("scalar", "mm", cbr_mm)
            emit("scalar", lambda e: e.copy(out=sm("cbr", E), in_=pf[4][:, 0:E]), "act")
            consumer_frees(4, "act", cnt["act"])
            return cnt["act"]

        def m5_batch(cbr_act):
            mall = g_bf.ap().bitcast(F32)[:, 0:NP * E]
            tA = g_bf.ap().bitcast(F32)[:, NP * E:NP * E + NP]
            tB = g_bf.ap().bitcast(F32)[:, NP * E + NP:NP * E + 2 * NP]
            tCi = g_bf.ap().bitcast(I32)[:, NP * E + 2 * NP:NP * E + 3 * NP]
            tDi = g_bf.ap().bitcast(I32)[:, NP * E + 3 * NP:NP * E + 4 * NP]
            g = "vector"
            wait(g, "act", cbr_act)
            wait(g, "act", m4_act)
            for pr in range(NP):
                tc, j = divmod(pr, 2)
                emit(g, lambda e, tc=tc, j=j: e.tensor_scalar(
                    out=mall[:, 0:E], in0=siota[:, :],
                    scalar1=s_rout[:, 8 * tc + j:8 * tc + j + 1],
                    scalar2=None, op0=OP.is_equal))
                emit(g, lambda e, pr=pr: e.scalar_tensor_tensor(
                    out=mall[:, E:2 * E], in0=sm("cbr", E), scalar=1.0,
                    in1=mall[:, 0:E], op0=OP.mult, op1=OP.mult,
                    accum_out=bass.AP(tA.tensor, tA.offset + pr,
                                      [list(tA.ap[0]), [1, 1]], None)))
            emit(g, lambda e: e.tensor_tensor(out=pairs_ap(4), in0=pairs_ap(4),
                                              in1=tA, op=OP.add))
            emit(g, lambda e: e.tensor_scalar(out=tA, in0=pairs_ap(0),
                                              scalar1=float(CAP), scalar2=None,
                                              op0=OP.mult))
            emit(g, lambda e: e.tensor_tensor(out=tB, in0=tA, in1=pairs_ap(4),
                                              op=OP.add))
            emit(g, lambda e: e.tensor_scalar(out=mall[:, 0:NP], in0=pairs_ap(4),
                                              scalar1=float(CAP) - 0.5, scalar2=None,
                                              op0=OP.is_gt))
            emit(g, lambda e: e.scalar_tensor_tensor(
                out=tB, in0=mall[:, 0:NP], scalar=float(2 * E * CAP), in1=tB,
                op0=OP.mult, op1=OP.add))
            emit(g, lambda e: e.tensor_copy(out=s_scat[:, :], in_=tB))
            emit(g, lambda e: e.tensor_copy(out=tCi, in_=pairs_ap(0)))
            emit(g, lambda e: e.tensor_scalar(out=tDi, in0=tCi, scalar1=1, scalar2=None,
                                              op0=OP.bitwise_and))
            emit(g, lambda e: e.tensor_scalar(out=tCi, in0=tCi, scalar1=1, scalar2=None,
                                              op0=OP.logical_shift_right))
            emit(g, lambda e: e.tensor_scalar(out=tCi, in0=tCi, scalar1=CAP, scalar2=None,
                                              op0=OP.mult))
            emit(g, lambda e: e.tensor_scalar(out=tDi, in0=tDi, scalar1=E * CAP // 2,
                                              scalar2=None, op0=OP.mult))
            emit(g, lambda e: e.tensor_tensor(out=tCi, in0=tCi, in1=tDi, op=OP.add))
            emit(g, lambda e: e.tensor_scalar(out=tB, in0=pairs_ap(4),
                                              scalar1=float(CAP - 1), scalar2=None,
                                              op0=OP.min))
            emit(g, lambda e: e.tensor_copy(out=tDi, in_=tB))
            emit(g, lambda e: e.tensor_tensor(out=s_gidx[:, :], in0=tCi, in1=tDi,
                                              op=OP.add))
            emit(g, lambda e: e.tensor_scalar(out=s_gidx2[:, :], in0=s_gidx[:, :],
                                              scalar1=E * CAP, scalar2=None,
                                              op0=OP.add), "ve")
            return cnt["ve"]

        def m5_scat_zero():
            return sdma("mz", "gpsimd",
                        lambda e: e.dma_start(out=map_in_t[0:MROW, :],
                                              in_=zeros[0:MROW, 0:128]))

        def m5_scat_grp(g, batch_ve, mzw):
            if g == 0:
                wait("gpsimd", "ve", batch_ve)
                wait("gpsimd", *mzw)
            for tc in (2 * g, 2 * g + 1):
                for j in (0, 1):
                    emit("gpsimd", lambda e, tc=tc, j=j: e.indirect_dma_start(
                        out=map_in_t.rearrange("a b -> (a b)").unsqueeze(1),
                        out_offset=IndirectOffsetOnAxis(
                            ap=s_scat[:, 2 * tc + j:2 * tc + j + 1], axis=0),
                        in_=s_sval[:, tc:tc + 1], in_offset=None), "mz", 16)
            if 2 * g + 2 >= NTC:
                wait("gpsimd", "mz", cnt["mz"])
                return cc_issue(lambda e: e.collective_compute(
                    "AllReduce", OP.add, replica_groups=[list(range(NCORE))],
                    ins=[map_in_t[0:MROW, :].opt()], outs=[map_out_t.opt()]))
            return None

        def e_map_post(mcc):
            wait("gpsimd", "cc", mcc)
            wait("gpsimd", "ve", m16_done)
            mlw = sdma("mz", "gpsimd", lambda e: e.indirect_dma_start(
                out=s_map16[:, :], out_offset=None,
                in_=map_out_t, in_offset=IndirectOffsetOnAxis(ap=s_m16[:, :], axis=0)))
            wait("tensor", *mlw)
            pe_take_bank(4)
            emit("tensor", lambda e: e.matmul(
                pf[4][:, 0:EPC * NSC], s_map16[:, :], sidnf[0:EPC * NSC, 0:EPC * NSC],
                start=True, stop=True), "mm")
            wait("scalar", "mm", cnt["mm"])
            emit("scalar", lambda e: e.copy(out=s_mapT[:, :], in_=pf[4][:, 0:EPC * NSC]),
                 "act")
            consumer_frees(4, "act", cnt["act"])
            mt = cnt["act"]
            wait("vector", "act", mt)
            emit("vector", lambda e: e.tensor_scalar(out=s_mapT[:, :], in0=s_mapT[:, :],
                                                     scalar1=-1.0, scalar2=0.0,
                                                     op0=OP.add, op1=OP.max))
            emit("vector", lambda e: e.tensor_copy(out=s_migx[:, :], in_=s_mapT[:, :]))
            HALF = TCORE // 2
            LH = HALF.bit_length() - 1
            LT = TCORE.bit_length() - 1
            W16 = EPC * NSC
            tA = u32(TCORE, W16).bitcast(I32)
            tB = u32(TCORE + W16, W16).bitcast(I32)
            emit("vector", lambda e: e.tensor_scalar(out=tA, in0=s_migx[:, :],
                                                     scalar1=LH, scalar2=None,
                                                     op0=OP.logical_shift_right))
            emit("vector", lambda e: e.tensor_scalar(out=tA, in0=tA, scalar1=1,
                                                     scalar2=None,
                                                     op0=OP.bitwise_and))
            emit("vector", lambda e: e.tensor_scalar(out=tA, in0=tA,
                                                     scalar1=NCORE * HALF,
                                                     scalar2=None, op0=OP.mult))
            emit("vector", lambda e: e.tensor_scalar(out=tB, in0=s_migx[:, :],
                                                     scalar1=LT, scalar2=None,
                                                     op0=OP.logical_shift_right))
            emit("vector", lambda e: e.tensor_scalar(out=tB, in0=tB, scalar1=HALF,
                                                     scalar2=None, op0=OP.mult))
            emit("vector", lambda e: e.tensor_tensor(out=tA, in0=tA, in1=tB,
                                                     op=OP.add))
            emit("vector", lambda e: e.tensor_scalar(out=tB, in0=s_migx[:, :],
                                                     scalar1=HALF - 1, scalar2=None,
                                                     op0=OP.bitwise_and))
            emit("vector", lambda e: e.tensor_tensor(out=s_migx[:, :], in0=tA, in1=tB,
                                                     op=OP.add), "ve")
            return ("ve", cnt["ve"])

        def e_dispatch_chunk(el, sc, ag1cc, migx):
            slot = sc % 2
            wait("gpsimd", "cc", ag1cc)
            wait("gpsimd", *migx)
            wait("gpsimd", "mm", xb_slot_free[slot])
            gmark = sdma("mz", "gpsimd",
                         lambda e, el=el, sc=sc, s=slot: e.indirect_dma_start(
                             out=xb16[:, s * D:(s + 1) * D], out_offset=None,
                             in_=ag1_out,
                             in_offset=IndirectOffsetOnAxis(
                                 ap=s_migx[:, el * NSC + sc:el * NSC + sc + 1],
                                 axis=0)))
            tve, tmm = transpose_tile(
                xb16[:, slot * D:],
                lambda dc, sc=sc: bigC[:, dc * CAP + sc * 128: dc * CAP + sc * 128 + 128],
                NKD, pre_wait=gmark)
            xb_slot_free[slot] = tmm
            return tve

        wait("tensor", "ve", m4_tpve[0])
        wait("scalar", "spl", cnt["spl"])
        jid_s = 0
        prev_gu_done = cnt["ve"]
        m5_state = {}
        disp0_ve = 0
        disp_cnt = [0]
        if NKIS >= 10:
            ic_counts, ic_cbr, ic_scat, ic_mappost = NTC + 1, NTC + 4, NTC + 5, NTC + 9
        else:
            ic_counts, ic_cbr, ic_scat, ic_mappost = NTC, NTC + 1, NTC + 1, NTC + 1
        for ic in range(NKIS):
            wsgs = load_slice(wsg_in[ic].rearrange("p k n -> p (k n)"), NKD * 128)
            gp, gv = proj_job(jid_s, wsgs,
                              lambda dc, tb: bigA[:, dc * TCORE + tb * TBS:
                                                  dc * TCORE + (tb + 1) * TBS],
                              TBS, NTB, NKD)
            jid_s += 1
            wsus = load_slice(wsu_in[ic].rearrange("p k n -> p (k n)"), NKD * 128)
            up, uv = proj_job(jid_s, wsus,
                              lambda dc, tb: bigA[:, dc * TCORE + tb * TBS:
                                                  dc * TCORE + (tb + 1) * TBS],
                              TBS, NTB, NKD)
            jid_s += 1
            wait("scalar", "ve", prev_gu_done)
            wait("scalar", "mm", gv)
            for tb in range(NTB):
                emit("scalar", lambda e, b=gp[tb % 2], tb=tb:
                     e.activation(out=hsl(A_, tb), in_=pf[b][:, :TBS], func=AF.Silu),
                     inc="act" if tb == NTB - 1 else None)
            gs_done = cnt["act"]
            for b in set(gp[i % 2] for i in range(NTB)):
                consumer_frees(b, "act", gs_done)
            wait("vector", "mm", uv)
            wait("vector", "act", gs_done)
            for tb in range(NTB):
                emit("vector", lambda e, b=up[tb % 2], tb=tb, ic=ic:
                     e.tensor_tensor(
                         out=bigB[:, ic * TCORE + tb * TBS: ic * TCORE + (tb + 1) * TBS],
                         in0=hsl(A_, tb), in1=pf[b][:, :TBS], op=OP.mult),
                     inc="ve" if tb == NTB - 1 else None)
            gu_done = cnt["ve"]
            prev_gu_done = gu_done
            for b in set(up[i % 2] for i in range(NTB)):
                consumer_frees(b, "ve", gu_done)
            if ic == 0:
                wait("gpsimd", *ag1_mark[NTC - 1])
                m5_state["acc"] = cc_issue(lambda e: e.collective_compute(
                    "AllGather", OP.bypass, replica_groups=[list(range(NCORE))],
                    ins=[ag1_in[TCORE // 2:TCORE, :].opt()],
                    outs=[ag1_out[NCORE * (TCORE // 2):, :].opt()]))
            if 1 <= ic <= NTC:
                routing_part2(ic - 1)
            if ic < NTC:
                routing_part1(ic)
            if ic == ic_counts:
                m5_state["cb"] = m5_counts()
            if ic == ic_cbr:
                m5_state["cbr"] = m5_cbr(m5_state["cb"])
                m5_state["bve"] = m5_batch(m5_state["cbr"])
                m5_state["mzw"] = m5_scat_zero()
            if ic_scat <= ic < ic_scat + (NTC + 1) // 2:
                r_ = m5_scat_grp(ic - ic_scat, m5_state["bve"], m5_state["mzw"])
                if r_ is not None:
                    m5_state["mcc"] = r_
            if ic == ic_mappost:
                m5_state["migx"] = e_map_post(m5_state["mcc"])
            if ic >= ic_mappost and disp_cnt[0] < NSC:
                disp0_ve = e_dispatch_chunk(0, disp_cnt[0],
                                            m5_state["acc"], m5_state["migx"])
                disp_cnt[0] += 1
        while disp_cnt[0] < NSC:
            disp0_ve = e_dispatch_chunk(0, disp_cnt[0],
                                        m5_state["acc"], m5_state["migx"])
            disp_cnt[0] += 1
        shared_gu_ve = cnt["ve"]

        eo_last = ("ag2s", 0)
        eo_marks = {}
        eo_ctr = [0]
        NGSL = (2 * D) // 512
        ag2cc = {}
        for el in range(EPC):
            if el > 0:
                bufT_ve = disp1_ve[-1] if disp1_ve else cnt["ve"]
            else:
                bufT_ve = disp0_ve
            wait("tensor", "ve", bufT_ve)
            jid_e = 0
            prev_gue = cnt["ve"]
            for icx in range(NKI):
                wge = load_slice(wg_in[el][icx].rearrange("p k n -> p (k n)"), NKD * 128)
                gp, gv2 = proj_job(jid_e, wge,
                                   lambda dc, tb: bigC[:, dc * CAP + tb * SBS:
                                                       dc * CAP + (tb + 1) * SBS],
                                   SBS, NSB, NKD)
                jid_e += 1
                wue = load_slice(wu_in[el][icx].rearrange("p k n -> p (k n)"), NKD * 128)
                up, uv2 = proj_job(jid_e, wue,
                                   lambda dc, tb: bigC[:, dc * CAP + tb * SBS:
                                                       dc * CAP + (tb + 1) * SBS],
                                   SBS, NSB, NKD)
                jid_e += 1
                wait("scalar", "ve", prev_gue)
                wait("scalar", "mm", gv2)
                for tb in range(NSB):
                    emit("scalar", lambda e, b=gp[tb % 2], tb=tb:
                         e.activation(out=U[:, tb * SBS:(tb + 1) * SBS],
                                      in_=pf[b][:, :SBS], func=AF.Silu),
                         inc="act" if tb == NSB - 1 else None)
                gs_done = cnt["act"]
                for b in set(gp[i % 2] for i in range(NSB)):
                    consumer_frees(b, "act", gs_done)
                wait("vector", "mm", uv2)
                wait("vector", "act", gs_done)
                for tb in range(NSB):
                    emit("vector", lambda e, b=up[tb % 2], tb=tb, icx=icx:
                         e.tensor_tensor(
                             out=bigA[:, icx * CAP + tb * SBS: icx * CAP + (tb + 1) * SBS],
                             in0=U[:, tb * SBS:(tb + 1) * SBS], in1=pf[b][:, :SBS],
                             op=OP.mult),
                         inc="ve" if tb == NSB - 1 else None)
                gu_done = cnt["ve"]
                prev_gue = gu_done
                for b in set(up[i % 2] for i in range(NSB)):
                    consumer_frees(b, "ve", gu_done)
            gue_ve = cnt["ve"]
            wait("tensor", "ve", gue_ve)
            wait("vector", "act", cnt["act"])
            disp1_ve = []
            for db in range(ND4):
                for i in range(NKI):
                    seq, wsl, wmark = load_slice(
                        wd_in[el][i][:, db * 512:(db + 1) * 512], 512)
                    wait("tensor", *wmark)
                    for sc in range(NSC):
                        if i == 0:
                            pe_take_bank(sc % 8)
                        emit("tensor",
                             lambda e, bk=sc % 8, i=i, sc=sc, w=wsl:
                             e.matmul(pf[bk][:, :],
                                      bigA[:, i * CAP + sc * 128: i * CAP + sc * 128 + 128],
                                      w[:, 0:512],
                                      start=(i == 0), stop=(i == NKI - 1)),
                             inc="mm" if sc == NSC - 1 else None)
                    slice_consumed(seq, "mm", cnt["mm"])
                mmdb = cnt["mm"]
                wait("vector", "mm", mmdb)
                for sc in range(NSC):
                    gsl = eo_ctr[0] % NGSL
                    eo_ctr[0] += 1
                    if eo_marks.get(gsl):
                        wait("vector", *eo_marks[gsl])
                    emit("vector", lambda e, bk=sc % 8, gsl=gsl: e.tensor_copy(
                        out=g_bf[:, gsl * 512:(gsl + 1) * 512], in_=pf[bk][:, :]),
                        inc="ve")
                    consumer_frees(sc % 8, "ve", cnt["ve"])
                    wait("sync", "ve", cnt["ve"])
                    hh = db // (ND4 // NH)
                    cb_ = db % (ND4 // NH)
                    eo_marks[gsl] = sdma("ag2s", "sync",
                                         lambda e, el=el, sc=sc, hh=hh, cb_=cb_,
                                         gsl=gsl:
                                         e.dma_start(
                                             out=ag2_in[hh * EPC * CAP + el * CAP
                                                        + sc * 128:
                                                        hh * EPC * CAP + el * CAP
                                                        + sc * 128 + 128,
                                                        cb_ * 512:(cb_ + 1) * 512],
                                             in_=g_bf[:, gsl * 512:(gsl + 1) * 512]))
                    eo_last = eo_marks[gsl]
                if el == 0:
                    for k in range(2):
                        scn = db * 2 + k
                        if scn < NSC:
                            disp1_ve.append(e_dispatch_chunk(
                                1, scn, m5_state["acc"], m5_state["migx"]))
                if (db + 1) % (ND4 // NH) == 0:
                    hh = db // (ND4 // NH)
                    wait("gpsimd", *eo_last)
                    ag2cc[(el, hh)] = cc_issue(
                        lambda e, el=el, hh=hh: e.collective_compute(
                            "AllGather", OP.bypass,
                            replica_groups=[list(range(NCORE))],
                            ins=[ag2_in[hh * EPC * CAP + el * CAP:
                                        hh * EPC * CAP + (el + 1) * CAP, :].opt()],
                            outs=[ag2_out[(hh * EPC + el) * NCORE * CAP:
                                          (hh * EPC + el + 1) * NCORE * CAP,
                                          :].opt()]))
        ag2_cc = ag2cc[(EPC - 1, NH - 1)]

        out_mark = {}
        fa_mark = {}
        gm_marks = {}
        stats_mark = {}
        xrm_mark = {}
        wait("tensor", "ve", shared_gu_ve)
        wait("vector", "act", cnt["act"])
        wait("vector", "spl", cnt["spl"])
        fdump = bigA.ap().bitcast(F32)[:, 0:D]

        def f_pair(k):
            j = k % 3
            if j == 0:
                return g_bf.ap()[:, 0:D], g_bf.ap()[:, D:2 * D]
            if j == 1:
                return xb16.ap()[:, 0:D], xb16.ap()[:, D:2 * D]
            return (zeros.ap().bitcast(BF16)[:, 0:D],
                    U[:, 2 * D:].bitcast(BF16)[:, 0:D])

        def f_gather(k):
            g0, g1 = f_pair(k)
            if k < 3:
                wait("gpsimd", "cc", ag2_cc)
            else:
                wait("gpsimd", *fa_mark[k - 3])
            for hh in range(NH):
                gx = s_gidx if hh == 0 else s_gidx2
                m0 = sdma("mz", "gpsimd",
                          lambda e, k=k, g0=g0, hh=hh, gx=gx: e.indirect_dma_start(
                              out=g0[:, hh * DH:(hh + 1) * DH], out_offset=None,
                              in_=ag2_out,
                              in_offset=IndirectOffsetOnAxis(
                                  ap=gx[:, 2 * k:2 * k + 1], axis=0)))
                m1 = sdma("mz2", "gpsimd",
                          lambda e, k=k, g1=g1, hh=hh, gx=gx: e.indirect_dma_start(
                              out=g1[:, hh * DH:(hh + 1) * DH], out_offset=None,
                              in_=ag2_out,
                              in_offset=IndirectOffsetOnAxis(
                                  ap=gx[:, 2 * k + 1:2 * k + 2], axis=0)))
            gm_marks[k] = (m0, m1)

        fb_apply_ve = {}

        def f_fa(k):
            par = k % 2
            Y = u32(par * D, D)
            r = 8 * k
            g0, g1 = f_pair(k)
            m0, m1 = gm_marks[k]
            wait("vector", *m0)
            wait("vector", *m1)
            emit("vector", lambda e, r=r, Y=Y, g0=g0, k=k: e.scalar_tensor_tensor(
                out=Y, in0=g0, scalar=s_rout[:, r + 2:r + 3],
                in1=bigC[:, k * D:(k + 1) * D], op0=OP.mult, op1=OP.add))
            emit("vector", lambda e, r=r, Y=Y, g1=g1: e.scalar_tensor_tensor(
                out=Y, in0=g1, scalar=s_rout[:, r + 3:r + 4],
                in1=Y, op0=OP.mult, op1=OP.add), "ve")
            fa_mark[k] = ("ve", cnt["ve"])
            if k + 3 < NTC:
                f_gather(k + 3)
            wait("scalar", "ve", cnt["ve"])
            emit("scalar", lambda e, Y=Y, par=par: e.activation(
                out=fdump, in_=Y, func=AF.Copy, accum_out=sm(f"paS{par}")))
            emit("scalar", lambda e, Y=Y, par=par: e.activation(
                out=fdump, in_=Y, func=AF.Square, accum_out=sm(f"paSS{par}")), "act")
            stats_mark[k] = cnt["act"]

        NSTG = (max(NKD * TCORE, NKI * CAP) // 2) // D - 1

        def f_fb(k):
            par = k % 2
            Y = u32(par * D, D)
            if NSTG >= 3:
                of = bigA.ap().bitcast(F32)[:, (1 + k % 3) * D:(2 + k % 3) * D]
            else:
                of = Y
            wait("vector", "act", stats_mark[k])
            stats_fin_g(f"paS{par}", f"paSS{par}", sm("s2"))
            if NSTG >= 3 and k >= 3:
                wait("vector", *out_mark[k - 3])
            if NSTG < 3 and k >= 2:
                wait("vector", *out_mark[k - 2])
            emit("vector", lambda e, Y=Y, of=of: e.tensor_scalar(
                out=of, in0=Y, scalar1=sm("mS"), scalar2=sm("s2"),
                op0=OP.subtract, op1=OP.mult), "ve")
            fb_apply_ve[k] = cnt["ve"]
            wait("vector", *xrm_mark[k])
            emit("vector", lambda e, of=of, par=par: e.tensor_tensor(
                out=of, in0=of, in1=sx[:, par * D:(par + 1) * D], op=OP.add), "ve")
            x_slot_free[par] = ("ve", cnt["ve"])
            wait("sync", "ve", cnt["ve"])
            out_mark[k] = sdma("outp", "sync",
                               lambda e, k=k, of=of: e.dma_start(
                                   out=y_out[k * 128:(k + 1) * 128, :], in_=of))
            if k + 2 < NTC:
                f_xrm(k + 2)

        def f_xrm(k):
            par = k % 2
            wait("sync", *x_slot_free[par])
            xrm_mark[k] = sdma(f"x{par}", "sync",
                               lambda e, k=k, par=par: e.dma_start(
                                   out=sx[:, par * D:(par + 1) * D],
                                   in_=xr1_sp[k * 128:(k + 1) * 128, :]))

        for db in range(ND4):
            for i in range(NKIS):
                seq, wsl, wmark = load_slice(
                    wsd_in[i][:, db * 512:(db + 1) * 512], 512)
                wait("tensor", *wmark)
                for tc in range(NTC):
                    if i == 0:
                        pe_take_bank(tc % 8)
                    emit("tensor",
                         lambda e, bk=tc % 8, i=i, tc=tc, w=wsl:
                         e.matmul(pf[bk][:, :],
                                  bigB[:, i * TCORE + tc * 128: i * TCORE + tc * 128 + 128],
                                  w[:, 0:512],
                                  start=(i == 0), stop=(i == NKIS - 1)),
                         inc="mm" if tc == NTC - 1 else None)
                slice_consumed(seq, "mm", cnt["mm"])
            mmdb = cnt["mm"]
            if db == ND4 - 1:
                for k in range(min(3, NTC)):
                    f_gather(k)
                f_xrm(0)
                f_xrm(1)
            wait("vector", "mm", mmdb)
            for tc in range(NTC):
                emit("vector", lambda e, bk=tc % 8, tc=tc, db=db: e.tensor_copy(
                    out=bigC[:, tc * D + db * 512: tc * D + db * 512 + 512],
                    in_=pf[bk][:, :]), "ve")
                consumer_frees(tc % 8, "ve", cnt["ve"])
        for k in range(NTC):
            f_fa(k)
            if k >= 1:
                f_fb(k - 1)
        f_fb(NTC - 1)
        if cfg.get("debug"):
            dbg_map = nc.dram_tensor("dbg_map", [MROW, 128], F32, kind="ExternalOutput").ap()
            dbg_ag2 = nc.dram_tensor("dbg_ag2", [NH * E * CAP, DH], BF16, kind="ExternalOutput").ap()
            dbg_xr1 = nc.dram_tensor("dbg_xr1", [TCORE, D], F32, kind="ExternalOutput").ap()
            dbg_ag1 = nc.dram_tensor("dbg_ag1", [TTOK, D], BF16, kind="ExternalOutput").ap()
            wait("sync", "outp", cnt["outp"])
            sdma("exr", "sync", lambda e: e.dma_start(out=dbg_map, in_=map_out_t))
            sdma("exr", "sync", lambda e: e.dma_start(out=dbg_ag2, in_=ag2_out))
            sdma("exr", "sync", lambda e: e.dma_start(out=dbg_xr1, in_=xr1_sp))
            sdma("exr", "sync", lambda e: e.dma_start(out=dbg_ag1, in_=ag1_out))
            wait("sync", "exr", cnt["exr"])
        wait("sync", "outp", cnt["outp"])

        block = ctx.enter_context(nc.Block())

        @block.sync
        def _(eng):
            for f in Q["sync"]:
                f(eng)

        @block.tensor
        def _(eng):
            for f in Q["tensor"]:
                f(eng)

        @block.vector
        def _(eng):
            for f in Q["vector"]:
                f(eng)

        @block.scalar
        def _(eng):
            for f in Q["scalar"]:
                f(eng)

        @block.gpsimd
        def _(eng):
            for f in Q["gpsimd"]:
                f(eng)

    return nc



def _prep_inputs(cfg, inp):
    D = cfg["D"]; TCORE = cfg["TCORE"]; CAP = cfg["CAP"]; E = cfg["E"]; NCORE = cfg["NCORE"]
    NKD = D // 128
    EPC = E // NCORE
    NSC = CAP // 128
    f32 = np.float32
    x = np.ascontiguousarray(np.asarray(inp["L0_attn_x"], dtype=f32).reshape(-1, D))

    def tile_lhsT(w):
        rows, colsn = w.shape
        return np.ascontiguousarray(
            w.reshape(rows // 128, 128, colsn // 128, 128).transpose(2, 1, 0, 3))

    def tile_rhs(w):
        return np.ascontiguousarray(w.reshape(w.shape[0] // 128, 128, w.shape[1]))

    mixW = {m: tile_lhsT(np.ascontiguousarray(np.asarray(inp[f"L0_attn_{m}_w"], f32).T))
            for m in ("A", "B", "D", "Ca", "Cb")}
    wo = tile_rhs(np.ascontiguousarray(np.asarray(inp["L0_attn_out_w"], f32).T))
    NNC = wo.shape[0]
    Ab = np.ascontiguousarray(np.asarray(inp["L0_attn_A_b"], f32).reshape(NNC, 128).T)
    rmsr = np.ascontiguousarray(np.broadcast_to(np.asarray(inp["L0_attn_rms_w"], f32),
                                                (128, D)))
    outbr = np.ascontiguousarray(np.broadcast_to(np.asarray(inp["L0_attn_out_b"], f32),
                                                 (128, D)))
    gate = np.asarray(inp["L0_ffn_gate_w"], f32)
    gatet = np.ascontiguousarray(gate.T.reshape(NKD, 128, E).transpose(1, 0, 2))
    wsg = tile_lhsT(np.asarray(inp["L0_ffn_ws_gate"], f32))
    wsu = tile_lhsT(np.asarray(inp["L0_ffn_ws_up"], f32))
    wsd = tile_rhs(np.asarray(inp["L0_ffn_ws_down"], f32))
    wg = np.asarray(inp["L0_ffn_w_gate"], f32)
    wu = np.asarray(inp["L0_ffn_w_up"], f32)
    wd = np.asarray(inp["L0_ffn_w_down"], f32)
    lt = np.triu(np.ones((128, 128), np.float32), 1).astype(ml_dtypes.bfloat16)
    idn = np.eye(128, dtype=ml_dtypes.bfloat16)
    idnf = np.eye(128, dtype=np.float32)

    in_maps = []
    for c in range(NCORE):
        im = dict(
            x_in=np.ascontiguousarray(x[c * TCORE:(c + 1) * TCORE]),
            wo=wo, Ab=Ab, rmsr=rmsr, outbr=outbr, gatet=gatet,
            wsg=wsg, wsu=wsu, wsd=wsd, lt=lt, idn=idn, idnf=idnf,
            tokb=np.full((128, 1), c * TCORE + 1, f32),
            mapb=np.full((128, 1), c * EPC * NSC, f32),
            pmr=np.ascontiguousarray(np.broadcast_to(
                np.array([1.0 if (cc // (NCORE // 2) == c // (NCORE // 2) and cc < c)
                          else 0.0 for cc in range(NCORE)], f32), (128, NCORE))),
            apmr=np.ascontiguousarray(np.broadcast_to(
                np.array([1.0 if cc < c else 0.0 for cc in range(NCORE)], f32),
                (128, NCORE))),
        )
        for m in ("A", "B", "D", "Ca", "Cb"):
            im[f"w{m}"] = mixW[m]
        for j in range(EPC):
            eidx = c * EPC + j
            im[f"wg{j}"] = tile_lhsT(wg[eidx])
            im[f"wu{j}"] = tile_lhsT(wu[eidx])
            im[f"wd{j}"] = tile_rhs(wd[eidx])
        in_maps.append(im)
    return in_maps


def kernel(**inputs):
    cfg = FULL
    in_maps = _prep_inputs(cfg, inputs)
    nc = build(cfg)
    res = run_bass_kernel_spmd(nc, in_maps, core_ids=list(range(cfg["NCORE"])))
    out = np.concatenate([r["y_out"] for r in res.results], axis=0)
    B, L, Dm = np.asarray(inputs["L0_attn_x"]).shape
    return out.reshape(B, L, Dm).astype(np.float32)
